# version 01
# baseline (speedup 1.0000x reference)
import numpy as np
import ml_dtypes
import concourse.bass as bass
import concourse.mybir as mybir
from concourse.bass_utils import run_bass_kernel_spmd

F32 = mybir.dt.float32
BF16 = mybir.dt.bfloat16
AF = mybir.ActivationFunctionType
ALU = mybir.AluOpType
AX = mybir.AxisListType
BF = ml_dtypes.bfloat16

D = 2048
DC = 16
T = 1024
H = 16
DH = 128
SEQ = 2048
EPS = 1e-5
QSCALE = DH ** -0.5
NEXP = 4
IEXP = 2048
NEG = -1.0e30

MODE = "fused"
DEBUG_DUMP = False

KB = 1024
PAGE = 2048


class Op:
    __slots__ = ("eng", "kind", "emit", "space", "inc_idx", "deps",
                 "snapshot", "waits", "tag", "hazard")

    def __init__(self, eng, kind, emit, tag="", hazard=False):
        self.eng, self.kind, self.emit, self.tag = eng, kind, emit, tag
        self.hazard = hazard
        self.space = None
        self.inc_idx = None
        self.deps = []
        self.snapshot = None
        self.waits = []


def _space_of(op):
    if op.kind == "cc":
        return "cc"
    if op.kind == "dma":
        return {"g": "gq", "y": "yq", "s": "sq"}[op.eng]
    return op.eng


class Sched:
    def __init__(self):
        self.ops = []
        self.writer = {}
        self.readers = {}

    def op(self, eng, emit, rd=(), wr=(), kind="compute", tag="", hazard=False):
        o = Op(eng, kind, emit, tag, hazard)
        deps = []
        for k in rd:
            w = self.writer.get(k)
            if w is not None:
                deps.append(w)
        for k in wr:
            w = self.writer.get(k)
            if w is not None:
                deps.append(w)
            deps.extend(self.readers.get(k, ()))
        pruned, seen = [], set()
        for d in deps:
            if d is o or id(d) in seen:
                continue
            seen.add(id(d))
            if d.eng == o.eng:
                if d.kind == "compute" and not d.hazard:
                    continue
                if d.kind == "dma" and o.kind == "dma":
                    continue
            pruned.append(d)
        o.deps = pruned
        for k in rd:
            self.readers.setdefault(k, []).append(o)
        for k in wr:
            self.writer[k] = o
            self.readers[k] = []
        self.ops.append(o)
        return o

    def finalize(self):
        needed = set()
        for o in self.ops:
            for d in o.deps:
                needed.add(id(d))
        counters = {}
        for o in self.ops:
            o.space = _space_of(o)
            if id(o) in needed or o.kind in ("dma", "cc"):
                counters[o.space] = counters.get(o.space, 0) + 1
                o.inc_idx = counters[o.space]
        self.space_totals = counters
        know = {e: {} for e in ("t", "v", "s", "g", "y")}
        for o in self.ops:
            k = know[o.eng]
            merged = None
            for d in o.deps:
                cur = merged if merged is not None else k
                if cur.get(d.space, 0) >= d.inc_idx:
                    continue
                if merged is None:
                    merged = dict(k)
                if d.snapshot:
                    for sp, v in d.snapshot.items():
                        if merged.get(sp, 0) < v:
                            merged[sp] = v
                if merged.get(d.space, 0) < d.inc_idx:
                    merged[d.space] = d.inc_idx
            if merged is not None:
                for sp, v in merged.items():
                    if k.get(sp, 0) < v:
                        o.waits.append((sp, v))
                know[o.eng] = merged
                k = merged
            if o.inc_idx is not None:
                snap = dict(k)
                snap[o.space] = max(snap.get(o.space, 0), o.inc_idx)
                o.snapshot = snap
            else:
                o.snapshot = k

    def make_bodies(self, sems):
        self.finalize()
        streams = {e: [o for o in self.ops if o.eng == e]
                   for e in ("t", "v", "s", "g", "y")}
        step = {sp: (16 if sp in ("gq", "yq", "sq") else 1) for sp in sems}

        def body(engname):
            def run(eng):
                for o in streams[engname]:
                    for sp, v in o.waits:
                        eng.wait_ge(sems[sp], v * step[sp])
                    inst = o.emit(eng)
                    if o.inc_idx is not None:
                        inst.then_inc(sems[o.space], step[o.space])
                for sp, tot in self.space_totals.items():
                    eng.wait_ge(sems[sp], tot * step[sp])
            return run
        return body


def sbk(off, size):
    return [("SB", p) for p in range(off // PAGE, (off + size - 1) // PAGE + 1)]


def drk(name, off=0, size=1 << 30):
    return [("D" + name, p) for p in range(off // 65536, (off + size - 1) // 65536 + 1)]


def build_program(mode):
    nc = bass.Bass("TRN2", target_bir_lowering=False)
    sc = Sched()
    do_l0 = mode in ("l0", "fused")
    do_l1 = mode in ("l1", "fused")

    def din(name, shape, dt=F32):
        return nc.dram_tensor(name, shape, dt, kind="ExternalInput").ap()

    x_in = din("x", [D, T]) if do_l0 else None
    if mode == "l0":
        xmid = nc.dram_tensor("xmid", [D, T], F32, kind="ExternalOutput").ap()
    elif mode == "l1":
        xmid = din("xmid", [D, T])
    else:
        xmid = nc.dram_tensor("xmid", [D, T], F32, kind="Internal").ap()
    out = (nc.dram_tensor("out", [D, T], F32, kind="ExternalOutput").ap()
           if do_l1 else None)

    w = {}
    if do_l0:
        for n, s in [("wq", [D, D]), ("wkv", [D, 256]), ("wo", [D, D]),
                     ("w1t", [D, 4096]), ("w2t", [4096, 4096]), ("w3t", [4096, D]),
                     ("b1r", [128, 32]), ("b2r", [128, 32]), ("b3r", [128, 16]),
                     ("gammar", [128, 16]), ("ropecd", [128, T]), ("ropesd", [128, T])]:
            w[n] = din(n, s)
        w["rtd"] = din("rtd", [128, 128], BF16)
    if do_l1:
        for n, s in [("wqkvt", [D, 3 * D]), ("woutt", [D, D]),
                     ("gup", [NEXP * D, 2 * IEXP]), ("dwn", [NEXP * IEXP, D]),
                     ("bqkvr", [128, 48]), ("boutr", [128, 16]),
                     ("alibicd", [128, 256])]:
            w[n] = din(n, s)
        w["arowd"] = din("arowd", [H, SEQ], BF16)
    w["maskkmd"] = din("maskkmd", [128, 256])
    w["maskqmd"] = din("maskqmd", [128, 256])
    w["onescbd"] = din("onescbd", [128, 1], BF16)
    w["onesrbd"] = din("onesrbd", [1, 128], BF16)
    w["onesrfd"] = din("onesrfd", [1, 128])
    w["idbd"] = din("idbd", [128, 128], BF16)

    ccd = {}
    if do_l0:
        ccd["k0i"] = nc.dram_tensor("k0i", [128, T], BF16, kind="Internal")
        ccd["k0o"] = nc.dram_tensor("k0o", [256, T], BF16, kind="Internal")
        ccd["v0i"] = nc.dram_tensor("v0i", [128, T], BF16, kind="Internal")
        ccd["v0o"] = nc.dram_tensor("v0o", [256, T], BF16, kind="Internal")
    if do_l1:
        ccd["k1i"] = nc.dram_tensor("k1i", [D, T], BF16, kind="Internal")
        ccd["k1o"] = nc.dram_tensor("k1o", [2 * D, T], BF16, kind="Internal")
        ccd["v1i"] = nc.dram_tensor("v1i", [D, T], BF16, kind="Internal")
        ccd["v1o"] = nc.dram_tensor("v1o", [2 * D, T], BF16, kind="Internal")

    BASE = 17 * KB
    A0, A1, A2, ST = BASE, BASE + 64 * KB, BASE + 128 * KB, BASE + 192 * KB

    def al(name, shape, dt, off):
        return nc.alloc_sbuf_tensor_at(name, shape, dt, offset=off).ap(), off

    off = ST
    ropec = ropes = rt = None
    if do_l0:
        ropec, _ = al("ropec", [128, T], F32, off); off += 4 * KB
        ropes, _ = al("ropes", [128, T], F32, off); off += 4 * KB
        rt, _ = al("rt", [128, 128], BF16, off); off += 256
    maskkm, _ = al("maskkm", [128, 256], F32, off); off += KB
    maskqm, _ = al("maskqm", [128, 256], F32, off); off += KB
    alibic = None
    if do_l1:
        alibic, _ = al("alibic", [128, 256], F32, off); off += KB
    onescb, _ = al("onescb", [128, 1], BF16, off); off += 32
    onesrb, _ = al("onesrb", [1, 128], BF16, off); off += 256
    onesrf, _ = al("onesrf", [1, 128], F32, off); off += 512
    idb, _ = al("idb", [128, 128], BF16, off); off += 256
    b1r = b2r = b3r = gammar = bqkvr = bqs = boutr = None
    if do_l0:
        b1r, _ = al("b1r", [128, 32], F32, off); off += 128
        b2r, _ = al("b2r", [128, 32], F32, off); off += 128
        b3r, _ = al("b3r", [128, 16], F32, off); off += 64
        gammar, _ = al("gammar", [128, 16], F32, off); off += 64
    if do_l1:
        bqkvr, _ = al("bqkvr", [128, 48], F32, off); off += 192
        bqs, _ = al("bqs", [128, 16], F32, off); off += 64
        boutr, _ = al("boutr", [128, 16], F32, off); off += 64
    assert off <= 229376 - 1024, off

    xn, XN_O = al("xn", [128, DC * 1024], BF16, A2)
    q, Q_O = al("q", [128, H * 1024], BF16, A2 + 32 * KB)
    xn2 = h1 = h2 = inter = None
    XN2_O = H1_O = H2_O = INTER_O = 0
    if do_l0:
        xn2, XN2_O = al("xn2", [128, DC * 1024], BF16, A1)
        h1, H1_O = al("h1", [128, 32 * 1024], BF16, A0)
        h2, H2_O = al("h2", [128, 32 * 1024], BF16, A1)
    if do_l1:
        inter, INTER_O = al("inter", [128, NEXP * 4096], BF16, A1 + 32 * KB)
    attn_fm, AFM_O = al("attn_fm", [128, H * 1024], BF16, A1 + 32 * KB)
    g_sb, G_O = al("g", [128, DC * 1024], F32, A0)

    def albuf(pfx, n, shape, dt, off0, stride):
        r = []
        for i in range(n):
            a, o = al(f"{pfx}{i}", shape, dt, off0 + i * stride)
            r.append((a, o))
        return r

    wstA = albuf("wstA", 2, [128, 4096], BF16, A0, 8 * KB)
    wstB = albuf("wstB", 2, [128, 4096], BF16, A1, 8 * KB)
    wstC = albuf("wstC", 2, [128, 4096], BF16, A2, 8 * KB)

    xinA = albuf("xinA", 2, [128, 1024], F32, A0 + 16 * KB, 4 * KB)
    tfA = albuf("tfA", 2, [128, 512], F32, A0 + 24 * KB, 2 * KB)
    rsc = albuf("rsc", 2, [128, 512], F32, A0 + 28 * KB, 2 * KB)
    qtmp, QT_O = al("qtmp", [128, 512], BF16, A0 + 32 * KB)
    ktmp, KT_O = al("ktmp", [128, 1024], BF16, A0 + 33 * KB)
    kvt = albuf("kvt", 2, [128, 1024], BF16, A0 + 36 * KB, 2 * KB)

    xinB = albuf("xinB", 2, [128, 1024], F32, A1, 4 * KB)
    x1b = albuf("x1b", 2, [128, 1024], F32, A1 + 8 * KB, 4 * KB)
    tfB = albuf("tfB", 2, [128, 512], F32, A1 + 16 * KB, 2 * KB)

    kb_ = albuf("kbuf", 2, [128, 2048], BF16, A1, 4 * KB)
    vb_ = albuf("vbuf", 2, [128, 2048], BF16, A1 + 8 * KB, 4 * KB)
    psb = albuf("psb", 2, [128, 512], BF16, A1 + 16 * KB, KB)
    avbf = albuf("avbf", 2, [128, 512], BF16, A1 + 18 * KB, KB)

    rowm, ROWM_O = al("rowmA", [128, 1], F32, A2 + 20 * KB)
    rowm2, ROWM2_O = al("rowm2", [128, 1], F32, A2 + 20 * KB + 64)
    negb, NEGB_O = al("negb", [128, 1], BF16, A2 + 20 * KB + 128)
    negrow, NEGROW_O = al("negrow", [1, 1024], BF16, A2 + 20 * KB + 256)
    invrow, INVROW_O = al("invrow", [1, 1024], F32, A2 + 20 * KB + 2304)
    arowb = albuf("arow", 2, [1, SEQ], BF16, A1 + 20 * KB, 4 * KB) if do_l1 else []
    xbf, XBF_O = al("xbf", [128, 1024], BF16, A2 + 16 * KB)
    sqbf, SQBF_O = al("sqbf", [128, 1024], BF16, A2 + 18 * KB)
    rows, ROWS_O = al("rows", [1, 4096], F32, A2 + 48 * KB)

    P = [nc.alloc_psum_tensor(f"P{i}", [128, 512], F32).ap() for i in range(8)]

    def PK(i):
        return [("PS", i)]

    xn3 = xn.rearrange("p (k t) -> p k t", k=DC)
    xn23 = xn2.rearrange("p (k t) -> p k t", k=DC) if do_l0 else None
    h13 = h1.rearrange("p (k t) -> p k t", k=32) if do_l0 else None
    h23 = h2.rearrange("p (k t) -> p k t", k=32) if do_l0 else None
    afm3 = attn_fm.rearrange("p (k t) -> p k t", k=H)
    g3 = g_sb.rearrange("p (k t) -> p k t", k=DC)
    int3 = inter.rearrange("p (e k t) -> p e k t", e=NEXP, k=IEXP // 128) if do_l1 else None
    q3 = q.rearrange("p (hh t) -> p hh t", hh=H)

    def dma(queue, out_ap, in_ap, rd, wr, tag=""):
        return sc.op(queue, lambda e, o=out_ap, i=in_ap: e.dma_start(out=o, in_=i),
                     rd=rd, wr=wr, kind="dma", tag=tag)

    def dma_t(out_ap, in_ap, rd, wr, tag=""):
        return sc.op("y", lambda e, o=out_ap, i=in_ap:
                     e.dma_start(out=o, in_=i, transpose=True),
                     rd=rd, wr=wr, kind="dma", tag=tag)

    def mm(out_ap, lhsT, rhs, rd, wr, start=True, stop=True, tag=""):
        return sc.op("t", lambda e, o=out_ap, l=lhsT, r=rhs, st=start, sp=stop:
                     e.matmul(o, l, r, start=st, stop=sp),
                     rd=rd, wr=wr, tag=tag)

    def act(out_ap, in_ap, func, rd, wr, bias=0.0, scale=1.0, accum=None, tag=""):
        return sc.op("s", lambda e, o=out_ap, i=in_ap, f=func, b=bias, sl=scale,
                     a=accum: e.activation(o, i, f, bias=b, scale=sl, accum_out=a),
                     rd=rd, wr=wr, tag=tag, hazard=True)

    def _hz(out_ap):
        return out_ap.free_size() <= 128

    def tt(out_ap, a_ap, b_ap, op, rd, wr, tag=""):
        return sc.op("v", lambda e, o=out_ap, a=a_ap, b=b_ap, p=op:
                     e.tensor_tensor(o, a, b, p), rd=rd, wr=wr, tag=tag,
                     hazard=_hz(out_ap))

    def ts(out_ap, in_ap, scalar, op, rd, wr, tag=""):
        return sc.op("v", lambda e, o=out_ap, i=in_ap, s_=scalar, p=op:
                     e.tensor_scalar(o, i, s_, None, p), rd=rd, wr=wr, tag=tag,
                     hazard=_hz(out_ap))

    def stt(out_ap, in0, scalar, in1, op0, op1, rd, wr, tag=""):
        return sc.op("v", lambda e, o=out_ap, i0=in0, s_=scalar, i1=in1, p0=op0,
                     p1=op1: e.scalar_tensor_tensor(o, i0, s_, i1, p0, p1),
                     rd=rd, wr=wr, tag=tag, hazard=_hz(out_ap))

    def vcopy(out_ap, in_ap, rd, wr, tag=""):
        return sc.op("v", lambda e, o=out_ap, i=in_ap: e.tensor_copy(o, i),
                     rd=rd, wr=wr, tag=tag, hazard=_hz(out_ap))

    def vredmax(out_ap, in_ap, rd, wr, tag=""):
        return sc.op("v", lambda e, o=out_ap, i=in_ap:
                     e.reduce_max(o, i, axis=AX.X), rd=rd, wr=wr, tag=tag,
                     hazard=True)

    def vrecip(out_ap, in_ap, rd, wr, tag=""):
        return sc.op("v", lambda e, o=out_ap, i=in_ap: e.reciprocal(o, i),
                     rd=rd, wr=wr, tag=tag, hazard=True)

    CONSTK = [("CONST", 0)]
    consts = [(maskkm, "maskkmd"), (maskqm, "maskqmd"), (onescb, "onescbd"),
              (onesrb, "onesrbd"), (onesrf, "onesrfd"), (idb, "idbd")]
    if do_l0:
        consts += [(ropec, "ropecd"), (ropes, "ropesd"), (rt, "rtd"),
                   (b1r, "b1r"), (b2r, "b2r"), (b3r, "b3r"), (gammar, "gammar")]
    if do_l1:
        consts += [(alibic, "alibicd"), (bqkvr, "bqkvr"), (boutr, "boutr")]
    for ap_, nm in consts:
        dma("g", ap_, w[nm], drk(nm), CONSTK, tag="const")
    if do_l1:
        ts(bqs, bqkvr[:, 0:16], QSCALE, ALU.mult, CONSTK, CONSTK, tag="bqs")

    def emit_stats_chunk(x_ap, x_keys, mc):
        act(xbf, x_ap, AF.Identity, x_keys, sbk(XBF_O, 2048), tag="xbf")
        tt(sqbf, xbf, xbf, ALU.mult, sbk(XBF_O, 2048), sbk(SQBF_O, 2048), tag="sq")
        for th in range(2):
            mm(P[4 + th][0:1, :], onescb, xbf[:, th * 512:(th + 1) * 512],
               sbk(XBF_O, 2048) + CONSTK, PK(4 + th),
               start=(mc == 0), stop=(mc == DC - 1), tag="s1")
            mm(P[6 + th][0:1, :], onescb, sqbf[:, th * 512:(th + 1) * 512],
               sbk(SQBF_O, 2048) + CONSTK, PK(6 + th),
               start=(mc == 0), stop=(mc == DC - 1), tag="s2")

    def emit_stats_from_dram(src, srcname):
        for mc in range(DC):
            xa, xo = xinA[mc % 2]
            dma("y", xa, src[mc * 128:(mc + 1) * 128, :],
                drk(srcname, mc * 128 * T * 4, 128 * T * 4), sbk(xo, 4096), tag="stx")
            emit_stats_chunk(xa, sbk(xo, 4096), mc)

    def emit_rowmath(l0):
        R = [rows[0:1, i * 1024:(i + 1) * 1024] for i in range(4)]
        RK = [sbk(ROWS_O + i * 4096, 4096) for i in range(4)]
        for th in range(2):
            vcopy(R[0][0:1, th * 512:(th + 1) * 512], P[4 + th][0:1, :],
                  PK(4 + th), RK[0], tag="s1cp")
            vcopy(R[1][0:1, th * 512:(th + 1) * 512], P[6 + th][0:1, :],
                  PK(6 + th), RK[1], tag="s2cp")
        ts(R[2], R[0], 1.0 / D, ALU.mult, RK[0], RK[2], tag="mu")
        ts(R[1], R[1], 1.0 / D, ALU.mult, RK[1], RK[1], tag="ms")
        tt(R[0], R[2], R[2], ALU.mult, RK[2], RK[0], tag="musq")
        tt(R[1], R[1], R[0], ALU.subtract, RK[0] + RK[1], RK[1], tag="var")
        ts(R[0], R[1], EPS, ALU.add, RK[1], RK[0], tag="vpe")
        act(R[0], R[0], AF.Ln, RK[0], RK[0], tag="ln")
        act(R[3], R[0], AF.Exp, RK[0], RK[3], scale=-0.5, tag="rstd")
        for th in range(2):
            mm(P[0 + th][:, :], onesrf, R[2][0:1, th * 512:(th + 1) * 512],
               RK[2] + CONSTK, PK(0 + th), tag="bmu")
            mm(P[2 + th][:, :], onesrf, R[3][0:1, th * 512:(th + 1) * 512],
               RK[3] + CONSTK, PK(2 + th), tag="brstd")
        if l0:
            tt(R[0], R[1], R[3], ALU.mult, RK[1] + RK[3], RK[0], tag="v2a")
            tt(R[0], R[0], R[3], ALU.mult, RK[0] + RK[3], RK[0], tag="v2b")
            ts(R[0], R[0], EPS, ALU.add, RK[0], RK[0], tag="v2c")
            act(R[0], R[0], AF.Ln, RK[0], RK[0], tag="ln2")
            act(R[0], R[0], AF.Exp, RK[0], RK[0], scale=-0.5, tag="rstd2")
            for th in range(2):
                mm(P[4 + th][:, :], onesrf, R[0][0:1, th * 512:(th + 1) * 512],
                   RK[0] + CONSTK, PK(4 + th), tag="brstd2")

    def emit_apply(src, srcname, l0):
        for mc in range(DC):
            xa, xo = xinA[mc % 2]
            dma("y", xa, src[mc * 128:(mc + 1) * 128, :],
                drk(srcname, mc * 128 * T * 4, 128 * T * 4), sbk(xo, 4096), tag="apx")
            for th in range(2):
                ta, to = tfA[th]
                tt(ta, xa[:, th * 512:(th + 1) * 512], P[0 + th][:, :], ALU.subtract,
                   sbk(xo, 4096) + PK(0 + th), sbk(to, 2048), tag="sub")
                tt(xn3[:, mc, th * 512:(th + 1) * 512], ta, P[2 + th][:, :], ALU.mult,
                   sbk(to, 2048) + PK(2 + th),
                   sbk(XN_O + mc * 2048 + th * 1024, 1024), tag="xn")
                if l0:
                    stt(xn23[:, mc, th * 512:(th + 1) * 512],
                        xn3[:, mc, th * 512:(th + 1) * 512],
                        gammar[:, mc:mc + 1], P[4 + th][:, :], ALU.mult, ALU.mult,
                        sbk(XN_O + mc * 2048 + th * 1024, 1024) + CONSTK + PK(4 + th),
                        sbk(XN2_O + mc * 2048 + th * 1024, 1024), tag="xn2")

    def xn_keys(k, t0, tn):
        return sbk(XN_O + k * 2048 + t0 * 2, tn * 2)

    def xn2_keys(k, t0, tn):
        return sbk(XN2_O + k * 2048 + t0 * 2, tn * 2)

    def afm_keys(k, t0, tn):
        return sbk(AFM_O + k * 2048 + t0 * 2, tn * 2)

    def wstream(wname, K, m0, m1, stripes, rhs3, rhs_keyfn, evac, mgroup=256,
                pbanks=(0, 1, 2, 3), row0=0):
        KC = K // 128
        rot = [0]
        wr3 = w[wname].rearrange("(k p) m -> p k m", p=128)
        kbase = row0 // 128
        for gi, g0 in enumerate(range(m0, m1, mgroup)):
            sa, so = stripes[gi % 2]
            s3 = sa.rearrange("p (k m) -> p k m", k=KC, m=mgroup)
            dma("g", s3, wr3[:, kbase:kbase + KC, g0:g0 + mgroup],
                drk(wname), sbk(so, 8192), tag="w:" + wname)
            for ml in range(mgroup // 128):
                mc_abs = (g0 + ml * 128) // 128
                for th in range(2):
                    t0, tn = th * 512, 512
                    pi = pbanks[rot[0] % len(pbanks)]
                    rot[0] += 1
                    for k in range(KC):
                        mm(P[pi][:, 0:tn], s3[:, k, ml * 128:(ml + 1) * 128],
                           rhs3[:, k, t0:t0 + tn],
                           sbk(so, 8192) + rhs_keyfn(k, t0, tn), PK(pi),
                           start=(k == 0), stop=(k == KC - 1), tag="mm:" + wname)
                    evac(mc_abs, th, t0, tn, pi)

    def emit_rope(src_ap, src_keys, dst_ap, dst_keys, pb):
        for th in range(2):
            pi = pb[th % len(pb)]
            mm(P[pi][:, :], rt, src_ap[:, th * 512:(th + 1) * 512],
               src_keys + CONSTK, PK(pi), tag="rot")
            ra, ro = rsc[0]
            tt(ra, src_ap[:, th * 512:(th + 1) * 512],
               ropec[:, th * 512:(th + 1) * 512], ALU.mult,
               src_keys + CONSTK, sbk(ro, 2048), tag="rc")
            ra2, ro2 = rsc[1]
            tt(ra2, P[pi][:, :], ropes[:, th * 512:(th + 1) * 512], ALU.mult,
               PK(pi) + CONSTK, sbk(ro2, 2048), tag="rs")
            tt(dst_ap[:, th * 512:(th + 1) * 512], ra, ra2, ALU.add,
               sbk(ro, 2048) + sbk(ro2, 2048), dst_keys, tag="radd")

    def emit_kvq_l0():
        sa, so = wstA[0]
        s3 = sa.rearrange("p (k m) -> p k m", k=DC, m=256)
        dma("g", s3, w["wkv"].rearrange("(k p) m -> p k m", p=128),
            drk("wkv"), sbk(so, 8192), tag="w:wkv")
        for th in range(2):
            pi = th
            for k in range(DC):
                mm(P[pi][:, :], s3[:, k, 0:128], xn23[:, k, th * 512:(th + 1) * 512],
                   sbk(so, 8192) + xn2_keys(k, th * 512, 512), PK(pi),
                   start=(k == 0), stop=(k == DC - 1), tag="mmk0")
            act(ktmp[:, th * 512:(th + 1) * 512], P[pi][:, :], AF.Identity,
                PK(pi), sbk(KT_O + th * 1024, 1024), tag="evk0")
        emit_rope(ktmp, sbk(KT_O, 2048), kvt[0][0], sbk(kvt[0][1], 2048), (2, 3))
        dma("y", ccd["k0i"].ap(), kvt[0][0], sbk(kvt[0][1], 2048),
            drk("k0i"), tag="k0w")
        for th in range(2):
            pi = 2 + th
            for k in range(DC):
                mm(P[pi][:, :], s3[:, k, 128:256], xn23[:, k, th * 512:(th + 1) * 512],
                   sbk(so, 8192) + xn2_keys(k, th * 512, 512), PK(pi),
                   start=(k == 0), stop=(k == DC - 1), tag="mmv0")
            act(kvt[1][0][:, th * 512:(th + 1) * 512], P[pi][:, :], AF.Identity,
                PK(pi), sbk(kvt[1][1] + th * 1024, 1024), tag="evv0")
        dma("y", ccd["v0i"].ap(), kvt[1][0], sbk(kvt[1][1], 2048),
            drk("v0i"), tag="v0w")

    def emit_q_l0():
        KC = DC
        wr3 = w["wq"].rearrange("(k p) m -> p k m", p=128)
        for gi, g0 in enumerate(range(0, D, 256)):
            sa, so = wstA[gi % 2]
            s3 = sa.rearrange("p (k m) -> p k m", k=KC, m=256)
            dma("g", s3, wr3[:, :, g0:g0 + 256], drk("wq"), sbk(so, 8192), tag="w:wq")
            for ml in range(2):
                h = (g0 + ml * 128) // 128
                for th in range(2):
                    pi = th
                    for k in range(KC):
                        mm(P[pi][:, :], s3[:, k, ml * 128:(ml + 1) * 128],
                           xn23[:, k, th * 512:(th + 1) * 512],
                           sbk(so, 8192) + xn2_keys(k, th * 512, 512), PK(pi),
                           start=(k == 0), stop=(k == KC - 1), tag="mmq0")
                    act(qtmp, P[pi][:, :], AF.Identity, PK(pi),
                        sbk(QT_O, 1024), scale=QSCALE, tag="evq0")
                    pi2 = 2 + th
                    mm(P[pi2][:, :], rt, qtmp, sbk(QT_O, 1024) + CONSTK, PK(pi2),
                       tag="rotq")
                    ra, ro = rsc[0]
                    tt(ra, qtmp, ropec[:, th * 512:(th + 1) * 512], ALU.mult,
                       sbk(QT_O, 1024) + CONSTK, sbk(ro, 2048), tag="rcq")
                    ra2, ro2 = rsc[1]
                    tt(ra2, P[pi2][:, :], ropes[:, th * 512:(th + 1) * 512], ALU.mult,
                       PK(pi2) + CONSTK, sbk(ro2, 2048), tag="rsq")
                    tt(q3[:, h, th * 512:(th + 1) * 512], ra, ra2, ALU.add,
                       sbk(ro, 2048) + sbk(ro2, 2048),
                       sbk(Q_O + h * 2048 + th * 1024, 1024), tag="raddq")

    def emit_kvq_l1():
        KC = DC
        wr3 = w["wqkvt"].rearrange("(k p) m -> p k m", p=128)
        for gi, g0 in enumerate(range(0, 3 * D, 256)):
            sa, so = wstA[gi % 2]
            s3 = sa.rearrange("p (k m) -> p k m", k=KC, m=256)
            dma("g", s3, wr3[:, :, g0:g0 + 256], drk("wqkvt"), sbk(so, 8192),
                tag="w:wqkvt")
            for ml in range(2):
                mc = (g0 + ml * 128) // 128
                kind = mc // 16
                mloc = mc % 16
                kv, kvo = kvt[mc % 2]
                for th in range(2):
                    pi = (mc * 2 + th) % 4
                    for k in range(KC):
                        mm(P[pi][:, :], s3[:, k, ml * 128:(ml + 1) * 128],
                           xn3[:, k, th * 512:(th + 1) * 512],
                           sbk(so, 8192) + xn_keys(k, th * 512, 512), PK(pi),
                           start=(k == 0), stop=(k == KC - 1), tag="mmqkv")
                    if kind == 0:
                        act(q3[:, mloc, th * 512:(th + 1) * 512], P[pi][:, :],
                            AF.Identity, PK(pi) + CONSTK,
                            sbk(Q_O + mloc * 2048 + th * 1024, 1024),
                            bias=bqs[:, mloc:mloc + 1], scale=QSCALE, tag="evq1")
                    else:
                        act(kv[:, th * 512:(th + 1) * 512], P[pi][:, :], AF.Identity,
                            PK(pi) + CONSTK, sbk(kvo + th * 1024, 1024),
                            bias=bqkvr[:, 16 * kind + mloc:16 * kind + mloc + 1],
                            tag="evkv1")
                if kind == 1:
                    dma("y", ccd["k1i"].ap()[mloc * 128:(mloc + 1) * 128, :], kv,
                        sbk(kvo, 2048), drk("k1i", mloc * 128 * T * 2, 128 * T * 2),
                        tag="k1w")
                elif kind == 2:
                    dma("y", ccd["v1i"].ap()[mloc * 128:(mloc + 1) * 128, :], kv,
                        sbk(kvo, 2048), drk("v1i", mloc * 128 * T * 2, 128 * T * 2),
                        tag="v1w")

    def emit_collective(ki, ko, vi, vo):
        for inm, onm in ((ki, ko), (vi, vo)):
            sc.op("g", lambda e, a=ccd[inm], b=ccd[onm]: e.collective_compute(
                "AllGather", ALU.bypass,
                replica_groups=[[0, 4], [1, 5], [2, 6], [3, 7]],
                ins=[a.ap().opt()], outs=[b.ap().opt()]),
                rd=drk(inm), wr=drk(onm), kind="cc", tag="cc:" + inm)

    def emit_ffn_l0():
        def ev_h1(mc, th, t0, tn, pi):
            act(h13[:, mc, t0:t0 + tn], P[pi][:, 0:tn], AF.Relu,
                PK(pi) + CONSTK, sbk(H1_O + mc * 2048 + t0 * 2, tn * 2),
                bias=b1r[:, mc:mc + 1], tag="evh1")
        wstream("w1t", D, 0, 4096, wstB, xn3, xn_keys, ev_h1)

        def h1k(k, t0, tn):
            return sbk(H1_O + k * 2048 + t0 * 2, tn * 2)

        def ev_h2(mc, th, t0, tn, pi):
            act(h23[:, mc, t0:t0 + tn], P[pi][:, 0:tn], AF.Relu,
                PK(pi) + CONSTK, sbk(H2_O + mc * 2048 + t0 * 2, tn * 2),
                bias=b2r[:, mc:mc + 1], tag="evh2")
        wstream("w2t", 4096, 0, 4096, wstC, h13, h1k, ev_h2, mgroup=128)

        def h2k(k, t0, tn):
            return sbk(H2_O + k * 2048 + t0 * 2, tn * 2)

        def ev_g(mc, th, t0, tn, pi):
            act(g3[:, mc, t0:t0 + tn], P[pi][:, 0:tn], AF.Silu,
                PK(pi) + CONSTK, sbk(G_O + mc * 4096 + t0 * 4, tn * 4),
                bias=b3r[:, mc:mc + 1], tag="evg")
        wstream("w3t", 4096, 0, D, wstC, h23, h2k, ev_g, mgroup=128)

    def emit_ffn_l1():
        IC = IEXP // 128
        wg3 = w["gup"].rearrange("(k p) m -> p k m", p=128)
        for e_ in range(NEXP):
            for gi, j0 in enumerate(range(0, IEXP, 256)):
                sa, so = wstB[0]
                sau, sou = wstB[1]
                sg3 = sa.rearrange("p (k m) -> p k m", k=DC, m=256)
                su3 = sau.rearrange("p (k m) -> p k m", k=DC, m=256)
                dma("g", sg3, wg3[:, e_ * DC:(e_ + 1) * DC, j0:j0 + 256],
                    drk("gup"), sbk(so, 8192), tag="w:gupg")
                dma("g", su3, wg3[:, e_ * DC:(e_ + 1) * DC,
                               IEXP + j0:IEXP + j0 + 256],
                    drk("gup"), sbk(sou, 8192), tag="w:gupu")
                for jl in range(2):
                    jc = (j0 + jl * 128) // 128
                    pg, pu = (0, 1) if (jc % 2 == 0) else (2, 3)
                    for k in range(DC):
                        mm(P[pg][:, 0:256], sg3[:, k, jl * 128:(jl + 1) * 128],
                           xn3[:, k, e_::4],
                           sbk(so, 8192) + xn_keys(k, 0, 1024), PK(pg),
                           start=(k == 0), stop=(k == DC - 1), tag="mmgate")
                    for k in range(DC):
                        mm(P[pu][:, 0:256], su3[:, k, jl * 128:(jl + 1) * 128],
                           xn3[:, k, e_::4],
                           sbk(sou, 8192) + xn_keys(k, 0, 1024), PK(pu),
                           start=(k == 0), stop=(k == DC - 1), tag="mmup")
                    pb, pbo = psb[0]
                    act(pb[:, 0:256], P[pg][:, 0:256], AF.Silu, PK(pg),
                        sbk(pbo, 512), tag="evgate")
                    tt(int3[:, e_, jc, :], pb[:, 0:256], P[pu][:, 0:256], ALU.mult,
                       sbk(pbo, 512) + PK(pu),
                       sbk(INTER_O + e_ * 8192 + jc * 512, 512), tag="inter")
        wd3 = w["dwn"].rearrange("(k p) m -> p k m", p=128)
        for e_ in range(NEXP):
            for gi, m0 in enumerate(range(0, D, 256)):
                sa, so = wstB[gi % 2]
                s3 = sa.rearrange("p (k m) -> p k m", k=IC, m=256)
                dma("g", s3, wd3[:, e_ * IC:(e_ + 1) * IC, m0:m0 + 256],
                    drk("dwn"), sbk(so, 8192), tag="w:dwn")
                for ml in range(2):
                    mc = (m0 + ml * 128) // 128
                    pi = mc % 4
                    for k in range(IC):
                        mm(P[pi][:, 0:256], s3[:, k, ml * 128:(ml + 1) * 128],
                           int3[:, e_, k, :],
                           sbk(so, 8192) + sbk(INTER_O + e_ * 8192 + k * 512, 512),
                           PK(pi), start=(k == 0), stop=(k == IC - 1), tag="mmdwn")
                    act(g3[:, mc, e_::4], P[pi][:, 0:256], AF.Silu,
                        PK(pi), sbk(G_O + mc * 4096, 4096), tag="evg1")

    def emit_attention(layer):
        l1 = layer == 1
        kon = "k1o" if l1 else "k0o"
        von = "v1o" if l1 else "v0o"
        ko = ccd[kon].ap()
        vo = ccd[von].ap()
        NEGROWK = sbk(NEGROW_O, 2048)
        INVK = sbk(INVROW_O, 4096)

        def load_kv(h, kbuf, kko, vbuf, vko):
            for r in range(2):
                src = (ko[r * D + h * 128:r * D + h * 128 + 128, :] if l1
                       else ko[r * 128:(r + 1) * 128, :])
                dma("y", kbuf[:, r * 1024:(r + 1) * 1024], src,
                    drk(kon), sbk(kko + r * 2048, 2048), tag="kld")
            for c in range(16):
                s_k, r = c // 2, c % 2
                src = (vo[r * D + h * 128:r * D + h * 128 + 128,
                          s_k * 128:(s_k + 1) * 128] if l1
                       else vo[r * 128:(r + 1) * 128, s_k * 128:(s_k + 1) * 128])
                dma_t(vbuf[:, c * 128:(c + 1) * 128], src,
                      drk(von), sbk(vko + c * 256, 256), tag="vld")

        if not l1:
            load_kv(0, kb_[0][0], kb_[0][1], vb_[0][0], vb_[0][1])

        for h in range(H):
            if l1:
                kbuf, kko = kb_[h % 2]
                vbuf, vko = vb_[h % 2]
                load_kv(h, kbuf, kko, vbuf, vko)
                ab, abo = arowb[h % 2]
                dma("y", ab, w["arowd"][h:h + 1, :], drk("arowd"),
                    sbk(abo, 4096), tag="arow")
            else:
                kbuf, kko = kb_[0]
                vbuf, vko = vb_[0]
            KBK = sbk(kko, 4096)
            VBK = sbk(vko, 4096)

            for s in range(8):
                qsl = q3[:, h, s * 128:(s + 1) * 128]
                QSK = sbk(Q_O + h * 2048 + s * 256, 256)
                first = True
                for r in range(2):
                    cap = (s + 1) * 128
                    for k0 in range(0, cap, 512):
                        kn = min(512, cap - k0)
                        pi = (k0 // 512 + r) % 2
                        mm(P[pi][:, 0:kn], qsl,
                           kbuf[:, r * 1024 + k0:r * 1024 + k0 + kn],
                           QSK + KBK, PK(pi), start=True, stop=not l1, tag="qmj")
                        if l1:
                            mm(P[pi][:, 0:kn], onesrb,
                               ab[0:1, r * 1024 + k0:r * 1024 + k0 + kn],
                               CONSTK + sbk(abo, 4096), PK(pi),
                               start=False, stop=True, tag="qmja")
                        if k0 + kn == cap:
                            tt(P[pi][:, kn - 128:kn], P[pi][:, kn - 128:kn],
                               maskqm[:, r * 128:(r + 1) * 128], ALU.add,
                               PK(pi) + CONSTK, PK(pi), tag="qmmask")
                        if first:
                            vredmax(rowm, P[pi][:, 0:kn], PK(pi),
                                    sbk(ROWM_O, 64), tag="qmred")
                        else:
                            vredmax(rowm2, P[pi][:, 0:kn], PK(pi),
                                    sbk(ROWM2_O, 64), tag="qmred2")
                            tt(rowm, rowm, rowm2, ALU.max,
                               sbk(ROWM_O, 64) + sbk(ROWM2_O, 64),
                               sbk(ROWM_O, 64), tag="qmcmb")
                        first = False
                if DEBUG_DUMP and l1 and h == 0 and s == 0:
                    for rr_ in range(2):
                        dq = nc.dram_tensor(f"dbg_qmj{rr_}", [128, 128], F32,
                                            kind="ExternalOutput").ap()
                        scr = nc.alloc_sbuf_tensor_at(
                            f"dbg_scr{rr_}", [128, 128], F32,
                            offset=A1 + 24 * KB + rr_ * 512).ap()
                        act(scr, P[rr_][:, 0:128], AF.Identity, PK(rr_),
                            sbk(A1 + 24 * KB + rr_ * 512, 512), tag="dbgev")
                        dma("y", dq, scr, sbk(A1 + 24 * KB + rr_ * 512, 512),
                            drk(f"dbg_qmj{rr_}"), tag="dbgq")
                    drm = nc.dram_tensor("dbg_rowm", [128, 1], F32,
                                         kind="ExternalOutput").ap()
                    dma("y", drm, rowm, sbk(ROWM_O, 64), drk("dbg_rowm"),
                        tag="dbgrm")
                ts(negb, rowm, -1.0, ALU.mult, sbk(ROWM_O, 64), sbk(NEGB_O, 64),
                   tag="negb")
                pi = 6 + (s % 2)
                mm(P[pi][0:1, 0:128], negb, idb, sbk(NEGB_O, 64) + CONSTK, PK(pi),
                   tag="negt")
                act(negrow[0:1, s * 128:(s + 1) * 128], P[pi][0:1, 0:128],
                    AF.Identity, PK(pi), NEGROWK, tag="negev")

            if DEBUG_DUMP and l1 and h == 0:
                dnr = nc.dram_tensor("dbg_negrow", [1, 1024], BF16,
                                     kind="ExternalOutput").ap()
                dma("y", dnr, negrow, NEGROWK, drk("dbg_negrow"), tag="dbgnr")

            for c in range(16):
                s_k, r = c // 2, c % 2
                base = r * 1024 + s_k * 128
                q0full = s_k * 128
                pieces = []
                p0 = q0full
                while p0 < 1024:
                    pn = min(512 - p0 % 512, 1024 - p0)
                    pieces.append((p0, pn))
                    p0 += pn
                for (q0, pn) in pieces:
                    pi = (c + (q0 // 512)) % 2
                    mm(P[pi][:, 0:pn], kbuf[:, base:base + 128],
                       q3[:, h, q0:q0 + pn],
                       KBK + sbk(Q_O + h * 2048 + q0 * 2, pn * 2), PK(pi),
                       start=True, stop=False, tag="kmj")
                    mm(P[pi][:, 0:pn], onesrb, negrow[0:1, q0:q0 + pn],
                       CONSTK + NEGROWK, PK(pi), start=False, stop=True, tag="kmjm")
                    if q0 == q0full:
                        tt(P[pi][:, 0:128], P[pi][:, 0:128],
                           maskkm[:, r * 128:(r + 1) * 128], ALU.add,
                           PK(pi) + CONSTK, PK(pi), tag="kmmask")
                    pb, pbo = psb[pi]
                    bias = alibic[:, h * 16 + c:h * 16 + c + 1] if l1 else 0.0
                    act(pb[:, 0:pn], P[pi][:, 0:pn], AF.Exp, PK(pi) + CONSTK,
                        sbk(pbo, 1024), bias=bias, tag="exp")
                    half = q0 // 512
                    qq = q0 % 512
                    mm(P[4 + half][0:1, qq:qq + pn], onescb, pb[:, 0:pn],
                       CONSTK + sbk(pbo, 1024), PK(4 + half),
                       start=(c == 0), stop=(c == 15), tag="rsum")
                    mm(P[2 + half][:, qq:qq + pn], vbuf[:, c * 128:(c + 1) * 128],
                       pb[:, 0:pn], VBK + sbk(pbo, 1024), PK(2 + half),
                       start=(c == 0), stop=(c == 15), tag="av")
            for half in range(2):
                vrecip(invrow[0:1, half * 512:(half + 1) * 512], P[4 + half][0:1, :],
                       PK(4 + half), INVK, tag="recip")
                pi = half
                mm(P[pi][:, :], onesrf, invrow[0:1, half * 512:(half + 1) * 512],
                   CONSTK + INVK, PK(pi), tag="binv")
                ab_, abo_ = avbf[half]
                act(ab_, P[2 + half][:, :], AF.Identity, PK(2 + half),
                    sbk(abo_, 1024), tag="avev")
                tt(afm3[:, h, half * 512:(half + 1) * 512], ab_, P[pi][:, :],
                   ALU.mult, sbk(abo_, 1024) + PK(pi),
                   sbk(AFM_O + h * 2048 + half * 1024, 1024), tag="aout")
            if DEBUG_DUMP and l1 and h == 0:
                dir_ = nc.dram_tensor("dbg_invrow", [1, 1024], F32,
                                      kind="ExternalOutput").ap()
                dma("y", dir_, invrow, INVK, drk("dbg_invrow"), tag="dbgir")

    def emit_combine(layer, x_src, srcname, x1_dst, dstname, fuse_stats):
        l1 = layer == 1
        wname = "woutt" if l1 else "wo"
        KC = DC
        wr3 = w[wname].rearrange("(k p) m -> p k m", p=128)
        for gi, g0 in enumerate(range(0, D, 256)):
            sa, so = wstC[gi % 2]
            s3 = sa.rearrange("p (k m) -> p k m", k=KC, m=256)
            dma("g", s3, wr3[:, :, g0:g0 + 256], drk(wname), sbk(so, 8192),
                tag="w:" + wname)
            for ml in range(2):
                mc = (g0 + ml * 128) // 128
                xa, xo = xinB[mc % 2]
                dma("y", xa, x_src[mc * 128:(mc + 1) * 128, :],
                    drk(srcname, mc * 128 * T * 4, 128 * T * 4), sbk(xo, 4096),
                    tag="cbx")
                xb, xbo = x1b[mc % 2]
                for th in range(2):
                    pi = (mc * 2 + th) % 4
                    for k in range(KC):
                        mm(P[pi][:, :], s3[:, k, ml * 128:(ml + 1) * 128],
                           afm3[:, k, th * 512:(th + 1) * 512],
                           sbk(so, 8192) + afm_keys(k, th * 512, 512), PK(pi),
                           start=(k == 0), stop=(k == KC - 1), tag="mmwo")
                    ta, to = tfB[th]
                    bias = boutr[:, mc:mc + 1] if l1 else 0.0
                    stt(ta, P[pi][:, :], bias, g3[:, mc, th * 512:(th + 1) * 512],
                        ALU.add, ALU.mult,
                        PK(pi) + CONSTK + sbk(G_O + mc * 4096 + th * 2048, 2048),
                        sbk(to, 2048), tag="sg")
                    tt(xb[:, th * 512:(th + 1) * 512], ta,
                       xa[:, th * 512:(th + 1) * 512], ALU.add,
                       sbk(to, 2048) + sbk(xo, 4096),
                       sbk(xbo + th * 2048, 2048), tag="x1")
                dma("y", x1_dst[mc * 128:(mc + 1) * 128, :], xb, sbk(xbo, 4096),
                    drk(dstname, mc * 128 * T * 4, 128 * T * 4), tag="x1w")
                if fuse_stats:
                    emit_stats_chunk(xb, sbk(xbo, 4096), mc)

    if do_l0:
        emit_stats_from_dram(x_in, "x")
        emit_rowmath(l0=True)
        emit_apply(x_in, "x", l0=True)
        emit_kvq_l0()
        emit_collective("k0i", "k0o", "v0i", "v0o")
        emit_q_l0()
        emit_ffn_l0()
        emit_attention(0)
        emit_combine(0, x_in, "x", xmid, "xmid", fuse_stats=(mode == "fused"))
    if do_l1:
        if mode == "l1":
            emit_stats_from_dram(xmid, "xmid")
        emit_rowmath(l0=False)
        emit_apply(xmid, "xmid", l0=False)
        emit_kvq_l1()
        emit_collective("k1i", "k1o", "v1i", "v1o")
        emit_ffn_l1()
        emit_attention(1)
        emit_combine(1, xmid, "xmid", out, "out", fuse_stats=False)

    if DEBUG_DUMP:
        dumps = [("xn", xn, BF16, DC * 1024, XN_O),
                 ("q", q, BF16, H * 1024, Q_O),
                 ("attn_fm", attn_fm, BF16, H * 1024, AFM_O),
                 ("g", g_sb, F32, DC * 1024, G_O)]
        for nm, ap_, dt, ncols, o_ in dumps:
            dd = nc.dram_tensor("dbg_" + nm, [128, ncols], dt,
                                kind="ExternalOutput").ap()
            dma("y", dd, ap_, sbk(o_, ncols * (2 if dt == BF16 else 4)),
                drk("dbg_" + nm), tag="dbg")

    from contextlib import ExitStack
    with ExitStack() as stack:
        sems = {}
        for sp in ("t", "v", "s", "g", "y", "gq", "yq", "sq", "cc"):
            sems[sp] = stack.enter_context(nc.semaphore(sp))
        block = stack.enter_context(nc.Block())
        body = sc.make_bodies(sems)
        block.tensor(body("t"))
        block.vector(body("v"))
        block.scalar(body("s"))
        block.gpsimd(body("g"))
        block.sync(body("y"))
    nc._sched = sc
    return nc


def _prep_shared(inp):
    f = np.float32
    sh = {}
    sh["wq"] = np.ascontiguousarray(inp["L0_attn_Wq"], f)
    sh["wkv"] = np.ascontiguousarray(inp["L0_attn_Wkv"], f)
    sh["wo"] = np.ascontiguousarray(inp["L0_attn_Wo"], f)
    sh["w1t"] = np.ascontiguousarray(np.asarray(inp["L0_ffn_W1"], f).T)
    sh["w2t"] = np.ascontiguousarray(np.asarray(inp["L0_ffn_W2"], f).T)
    sh["w3t"] = np.ascontiguousarray(np.asarray(inp["L0_ffn_W3"], f).T)
    sh["b1r"] = np.ascontiguousarray(np.asarray(inp["L0_ffn_b1"], f).reshape(32, 128).T)
    sh["b2r"] = np.ascontiguousarray(np.asarray(inp["L0_ffn_b2"], f).reshape(32, 128).T)
    sh["b3r"] = np.ascontiguousarray(np.asarray(inp["L0_ffn_b3"], f).reshape(16, 128).T)
    sh["gammar"] = np.ascontiguousarray(np.asarray(inp["L0_attn_gamma"], f).reshape(16, 128).T)
    sh["wqkvt"] = np.ascontiguousarray(np.asarray(inp["L1_attn_Wqkv"], f).T)
    sh["woutt"] = np.ascontiguousarray(np.asarray(inp["L1_attn_Wout"], f).T)
    sh["gup"] = np.ascontiguousarray(np.asarray(inp["L1_ffn_gate_up_proj"], f)
                                     .reshape(NEXP * D, 2 * IEXP))
    sh["dwn"] = np.ascontiguousarray(np.asarray(inp["L1_ffn_down_proj"], f)
                                     .reshape(NEXP * IEXP, D))
    sh["bqkvr"] = np.ascontiguousarray(np.asarray(inp["L1_attn_b_qkv"], f).reshape(48, 128).T)
    sh["boutr"] = np.ascontiguousarray(np.asarray(inp["L1_attn_b_out"], f).reshape(16, 128).T)

    rtm = np.zeros((128, 128), f)
    for jj in range(64):
        rtm[64 + jj, jj] = -1.0
    for jj in range(64, 128):
        rtm[jj - 64, jj] = 1.0
    sh["rtd"] = rtm.astype(BF)

    sh["onescbd"] = np.ones((128, 1), BF)
    sh["onesrbd"] = np.ones((1, 128), BF)
    sh["onesrfd"] = np.ones((1, 128), f)
    sh["idbd"] = np.eye(128, dtype=BF)

    hh = np.arange(H)
    slopes = 2.0 ** (-(hh + 1) * 0.5)
    cpos = np.zeros((128, H * 16), f)
    for h in range(H):
        for c in range(16):
            s_k, r = c // 2, c % 2
            pos = (2 * s_k + r) * 128 + np.arange(128)
            cpos[:, h * 16 + c] = ((pos - (SEQ - 1)) * slopes[h]).astype(f)
    sh["alibicd"] = cpos
    ar = np.zeros((H, SEQ), np.float64)
    col = np.zeros(SEQ, np.int64)
    for r in range(2):
        for s in range(8):
            col[r * 1024 + s * 128:r * 1024 + (s + 1) * 128] = \
                (2 * s + r) * 128 + np.arange(128)
    for h in range(H):
        ar[h] = (col - (SEQ - 1)) * slopes[h]
    sh["arowd"] = ar.astype(BF)
    return sh


def _prep_core(inp, c):
    f = np.float32
    b, par = c % 4, c // 4
    tiles = [2 * s + par for s in range(8)]
    xs = np.asarray(inp["L0_attn_x"][b], f)
    xcore = np.ascontiguousarray(
        np.concatenate([xs[t * 128:(t + 1) * 128] for t in tiles], 0).T)
    d = {"x": xcore}
    pos = np.concatenate([np.arange(t * 128, (t + 1) * 128) for t in tiles])
    inv = 1.0 / (10000.0 ** (np.arange(0, DH, 2) / DH))
    fr = pos[:, None] * inv[None, :]
    fr = np.concatenate([fr, fr], 1)
    d["ropecd"] = np.ascontiguousarray(np.cos(fr).T.astype(f))
    d["ropesd"] = np.ascontiguousarray(np.sin(fr).T.astype(f))
    i_ = np.arange(128)[:, None]
    j_ = np.arange(128)[None, :]
    tri = np.where(i_ <= j_, 0.0, NEG).astype(f)
    zer = np.zeros((128, 128), f)
    full = np.full((128, 128), NEG, f)
    if par == 0:
        km = np.concatenate([tri, full], 1)
        qm = np.concatenate([np.ascontiguousarray(tri.T), full], 1)
    else:
        km = np.concatenate([zer, tri], 1)
        qm = np.concatenate([zer, np.ascontiguousarray(tri.T)], 1)
    d["maskkmd"] = np.ascontiguousarray(km)
    d["maskqmd"] = np.ascontiguousarray(qm)
    return d, b, tiles


def kernel(**inputs):
    shared = _prep_shared(inputs)
    per_core, metas = [], []
    for c in range(8):
        d, b, tiles = _prep_core(inputs, c)
        d.update(shared)
        per_core.append(d)
        metas.append((b, tiles))

    outfull = np.zeros((4, SEQ, D), np.float32)

    def scatter(res, key):
        for c in range(8):
            o = np.asarray(res.results[c][key]).T
            b, tiles = metas[c]
            for si, t in enumerate(tiles):
                outfull[b, t * 128:(t + 1) * 128, :] = o[si * 128:(si + 1) * 128]

    if MODE == "fused":
        nc = build_program("fused")
        res = run_bass_kernel_spmd(nc, per_core, core_ids=list(range(8)))
        scatter(res, "out")
    else:
        nc0 = build_program("l0")
        res0 = run_bass_kernel_spmd(nc0, per_core, core_ids=list(range(8)))
        nc1 = build_program("l1")
        maps1 = []
        for c in range(8):
            d = dict(per_core[c])
            d["xmid"] = np.ascontiguousarray(res0.results[c]["xmid"])
            maps1.append(d)
        res1 = run_bass_kernel_spmd(nc1, maps1, core_ids=list(range(8)))
        scatter(res1, "out")
    return outfull


# version 02
# speedup vs baseline: 1.0619x; 1.0619x over previous
import numpy as np
import ml_dtypes
import concourse.bass as bass
import concourse.mybir as mybir
from concourse.bass_utils import run_bass_kernel_spmd

F32 = mybir.dt.float32
BF16 = mybir.dt.bfloat16
AF = mybir.ActivationFunctionType
ALU = mybir.AluOpType
AX = mybir.AxisListType
BF = ml_dtypes.bfloat16

D = 2048
DC = 16
T = 1024
H = 16
DH = 128
SEQ = 2048
EPS = 1e-5
QSCALE = DH ** -0.5
NEXP = 4
IEXP = 2048
NEG = -1.0e30

MODE = "fused"
KM_NBANK = 4
NQSEM = 8
FFN_NEW = True
FFN_NEWBUF = True
DEBUG_DUMP = False

KB = 1024
PAGE = 2048


class Op:
    __slots__ = ("eng", "kind", "emit", "space", "inc_idx", "deps",
                 "snapshot", "waits", "tag", "hazard", "phase")

    def __init__(self, eng, kind, emit, tag="", hazard=False, phase=""):
        self.eng, self.kind, self.emit, self.tag = eng, kind, emit, tag
        self.hazard = hazard
        self.phase = phase
        self.space = None
        self.inc_idx = None
        self.deps = []
        self.snapshot = None
        self.waits = []


def _space_of(op):
    if op.kind == "cc":
        return "cc"
    if op.kind == "dma":
        return {"g": "gq", "y": "yq", "s": "sq"}[op.eng]
    return op.eng


class Sched:
    def __init__(self):
        self.ops = []
        self.writer = {}
        self.readers = {}
        self.cur_phase = ""

    def phase(self, name):
        self.cur_phase = name

    def op(self, eng, emit, rd=(), wr=(), kind="compute", tag="", hazard=False):
        o = Op(eng, kind, emit, tag, hazard, self.cur_phase)
        deps = []
        for k in rd:
            w = self.writer.get(k)
            if w is not None:
                deps.append(w)
        for k in wr:
            w = self.writer.get(k)
            if w is not None:
                deps.append(w)
            deps.extend(self.readers.get(k, ()))
        pruned, seen = [], set()
        for d in deps:
            if d is o or id(d) in seen:
                continue
            seen.add(id(d))
            if d.eng == o.eng:
                if d.kind == "compute" and not d.hazard:
                    continue
                if d.kind == "dma" and o.kind == "dma":
                    continue
            pruned.append(d)
        o.deps = pruned
        for k in rd:
            self.readers.setdefault(k, []).append(o)
        for k in wr:
            self.writer[k] = o
            self.readers[k] = []
        self.ops.append(o)
        return o

    def finalize(self):
        needed = set()
        for o in self.ops:
            for d in o.deps:
                needed.add(id(d))
        qrr = {}
        for o in self.ops:
            sp = _space_of(o)
            if o.kind == "dma":
                i = qrr.get(sp, 0)
                qrr[sp] = i + 1
                sp = f"{sp}{i % NQSEM}"
            o.space = sp
        counters = {}
        for o in self.ops:
            if id(o) in needed or o.kind in ("dma", "cc"):
                counters[o.space] = counters.get(o.space, 0) + 1
                o.inc_idx = counters[o.space]
        self.space_totals = counters
        know = {e: {} for e in ("t", "v", "s", "g", "y")}
        for o in self.ops:
            k = know[o.eng]
            merged = None
            for d in o.deps:
                cur = merged if merged is not None else k
                if cur.get(d.space, 0) >= d.inc_idx:
                    continue
                if merged is None:
                    merged = dict(k)
                if d.snapshot:
                    for sp, v in d.snapshot.items():
                        if merged.get(sp, 0) < v:
                            merged[sp] = v
                if merged.get(d.space, 0) < d.inc_idx:
                    merged[d.space] = d.inc_idx
            if merged is not None:
                for sp, v in merged.items():
                    if k.get(sp, 0) < v:
                        o.waits.append((sp, v))
                know[o.eng] = merged
                k = merged
            if o.inc_idx is not None:
                snap = dict(k)
                snap[o.space] = max(snap.get(o.space, 0), o.inc_idx)
                o.snapshot = snap
            else:
                o.snapshot = k

    def make_bodies(self, sems):
        self.finalize()
        streams = {e: [o for o in self.ops if o.eng == e]
                   for e in ("t", "v", "s", "g", "y")}
        step = {sp: (16 if sp[0:2] in ("gq", "yq", "sq") else 1) for sp in sems}

        def body(engname, nc=None):
            def run(eng):
                cur = [None]

                def emit_one(o):
                    for sp, v in o.waits:
                        eng.wait_ge(sems[sp], v * step[sp])
                    inst = o.emit(eng)
                    if o.inc_idx is not None:
                        inst.then_inc(sems[o.space], step[o.space])
                for o in streams[engname]:
                    if nc is not None and o.phase != cur[0]:
                        cur[0] = o.phase
                    if nc is not None and o.phase:
                        with nc.named_scope(o.phase):
                            emit_one(o)
                    else:
                        emit_one(o)
                for sp, tot in self.space_totals.items():
                    eng.wait_ge(sems[sp], tot * step[sp])
            return run
        return body


def sbk(off, size):
    return [("SB", p) for p in range(off // PAGE, (off + size - 1) // PAGE + 1)]


def drk(name, off=0, size=1 << 30):
    return [("D" + name, p) for p in range(off // 65536, (off + size - 1) // 65536 + 1)]


def build_program(mode):
    nc = bass.Bass("TRN2", target_bir_lowering=False)
    sc = Sched()
    do_l0 = mode in ("l0", "fused")
    do_l1 = mode in ("l1", "fused")

    def din(name, shape, dt=F32):
        return nc.dram_tensor(name, shape, dt, kind="ExternalInput").ap()

    x_in = din("x", [D, T]) if do_l0 else None
    if mode == "l0":
        xmid = nc.dram_tensor("xmid", [D, T], F32, kind="ExternalOutput").ap()
    elif mode == "l1":
        xmid = din("xmid", [D, T])
    else:
        xmid = nc.dram_tensor("xmid", [D, T], F32, kind="Internal").ap()
    out = (nc.dram_tensor("out", [D, T], F32, kind="ExternalOutput").ap()
           if do_l1 else None)

    w = {}
    if do_l0:
        for n, s in [("wq", [D, D]), ("wkv", [D, 256]), ("wo", [D, D]),
                     ("w1t", [D, 4096]), ("w2t", [4096, 4096]), ("w3t", [4096, D]),
                     ("b1r", [128, 32]), ("b2r", [128, 32]), ("b3r", [128, 16]),
                     ("gammar", [128, 16]), ("ropecd", [128, T]), ("ropesd", [128, T])]:
            w[n] = din(n, s)
        w["rtd"] = din("rtd", [128, 128], BF16)
    if do_l1:
        for n, s in [("wqkvt", [D, 3 * D]), ("woutt", [D, D]),
                     ("gup", [NEXP * D, 2 * IEXP]), ("dwn", [NEXP * IEXP, D]),
                     ("bqkvr", [128, 48]), ("boutr", [128, 16]),
                     ("alibicd", [128, 256])]:
            w[n] = din(n, s)
        w["arowd"] = din("arowd", [H, SEQ], BF16)
    w["maskkmd"] = din("maskkmd", [128, 256])
    w["maskqmd"] = din("maskqmd", [128, 256])
    w["onescbd"] = din("onescbd", [128, 1], BF16)
    w["onesrbd"] = din("onesrbd", [1, 128], BF16)
    w["onesrfd"] = din("onesrfd", [1, 128])
    w["idbd"] = din("idbd", [128, 128], BF16)

    ccd = {}
    if do_l0:
        ccd["k0i"] = nc.dram_tensor("k0i", [128, T], BF16, kind="Internal")
        ccd["k0o"] = nc.dram_tensor("k0o", [256, T], BF16, kind="Internal")
        ccd["v0i"] = nc.dram_tensor("v0i", [128, T], BF16, kind="Internal")
        ccd["v0o"] = nc.dram_tensor("v0o", [256, T], BF16, kind="Internal")
    if do_l1:
        ccd["k1i"] = nc.dram_tensor("k1i", [D, T], BF16, kind="Internal")
        ccd["k1o"] = nc.dram_tensor("k1o", [2 * D, T], BF16, kind="Internal")
        ccd["v1i"] = nc.dram_tensor("v1i", [D, T], BF16, kind="Internal")
        ccd["v1o"] = nc.dram_tensor("v1o", [2 * D, T], BF16, kind="Internal")

    BASE = 17 * KB
    A0, A1, A2, ST = BASE, BASE + 64 * KB, BASE + 128 * KB, BASE + 192 * KB

    def al(name, shape, dt, off):
        return nc.alloc_sbuf_tensor_at(name, shape, dt, offset=off).ap(), off

    off = ST
    ropec = ropes = rt = None
    if do_l0:
        ropec, _ = al("ropec", [128, T], F32, off); off += 4 * KB
        ropes, _ = al("ropes", [128, T], F32, off); off += 4 * KB
        rt, _ = al("rt", [128, 128], BF16, off); off += 256
    maskkm, _ = al("maskkm", [128, 256], F32, off); off += KB
    maskqm, _ = al("maskqm", [128, 256], F32, off); off += KB
    alibic = None
    if do_l1:
        alibic, _ = al("alibic", [128, 256], F32, off); off += KB
    onescb, _ = al("onescb", [128, 1], BF16, off); off += 32
    onesrb, _ = al("onesrb", [1, 128], BF16, off); off += 256
    onesrf, _ = al("onesrf", [1, 128], F32, off); off += 512
    idb, _ = al("idb", [128, 128], BF16, off); off += 256
    b1r = b2r = b3r = gammar = bqkvr = bqs = boutr = None
    if do_l0:
        b1r, _ = al("b1r", [128, 32], F32, off); off += 128
        b2r, _ = al("b2r", [128, 32], F32, off); off += 128
        b3r, _ = al("b3r", [128, 16], F32, off); off += 64
        gammar, _ = al("gammar", [128, 16], F32, off); off += 64
    if do_l1:
        bqkvr, _ = al("bqkvr", [128, 48], F32, off); off += 192
        bqs, _ = al("bqs", [128, 16], F32, off); off += 64
        boutr, _ = al("boutr", [128, 16], F32, off); off += 64
    assert off <= 229376 - 1024, off

    xn, XN_O = al("xn", [128, DC * 1024], BF16, A2)
    q, Q_O = al("q", [128, H * 1024], BF16, A2 + 32 * KB)
    xn2 = h1 = h2 = inter = None
    XN2_O = H1_O = H2_O = INTER_O = 0
    if do_l0:
        xn2, XN2_O = al("xn2", [128, DC * 1024], BF16, A1)
        h1, H1_O = al("h1", [128, 32 * 1024], BF16, A0)
        h2, H2_O = al("h2", [128, 32 * 1024], BF16, A1)
    if do_l1:
        inter, INTER_O = al("inter", [128, NEXP * 4096], BF16, A1 + 32 * KB)
    attn_fm, AFM_O = al("attn_fm", [128, H * 1024], BF16, A1 + 32 * KB)
    g_sb, G_O = al("g", [128, DC * 1024], F32, A0)

    def albuf(pfx, n, shape, dt, off0, stride):
        r = []
        for i in range(n):
            a, o = al(f"{pfx}{i}", shape, dt, off0 + i * stride)
            r.append((a, o))
        return r

    wstA = albuf("wstA", 2, [128, 4096], BF16, A0, 8 * KB)
    wstB = albuf("wstB", 2, [128, 4096], BF16, A1, 8 * KB)
    wstC = albuf("wstC", 2, [128, 4096], BF16, A2, 8 * KB)

    xinA = albuf("xinA", 2, [128, 1024], F32, A0 + 16 * KB, 4 * KB)
    tfA = albuf("tfA", 2, [128, 512], F32, A0 + 24 * KB, 2 * KB)
    rsc = albuf("rsc", 2, [128, 512], F32, A0 + 28 * KB, 2 * KB)
    qtmp, QT_O = al("qtmp", [128, 512], BF16, A0 + 32 * KB)
    ktmp, KT_O = al("ktmp", [128, 1024], BF16, A0 + 33 * KB)
    kvt = albuf("kvt", 2, [128, 1024], BF16, A0 + 36 * KB, 2 * KB)

    xinB = albuf("xinB", 2, [128, 1024], F32, A1, 4 * KB)
    x1b = albuf("x1b", 2, [128, 1024], F32, A1 + 8 * KB, 4 * KB)
    tfB = albuf("tfB", 2, [128, 512], F32, A1 + 16 * KB, 2 * KB)

    kb_ = albuf("kbuf", 2, [128, 2048], BF16, A1, 4 * KB)
    vb_ = albuf("vbuf", 2, [128, 2048], BF16, A1 + 8 * KB, 4 * KB)
    psb = albuf("psb", 4, [128, 512], BF16, A1 + 16 * KB, KB)
    avbf = albuf("avbf", 2, [128, 512], BF16, A1 + 28 * KB, KB)
    gupst = albuf("gupst", 4, [128, 4096], BF16, A0, 8 * KB) if do_l1 else []

    pmax, PMAX_O = al("pmax", [128, 32], F32, A2 + 20 * KB)
    negbs, NEGBS_O = al("negbs", [128, 8], BF16, A2 + 20 * KB + 128)
    negrow, NEGROW_O = al("negrow", [1, 1024], BF16, A2 + 20 * KB + 256)
    invrow, INVROW_O = al("invrow", [1, 1024], F32, A2 + 20 * KB + 2304)
    arowb = albuf("arow", 2, [1, SEQ], BF16, A1 + 20 * KB, 4 * KB) if do_l1 else []
    xbf, XBF_O = al("xbf", [128, 1024], BF16, A2 + 16 * KB)
    sqbf, SQBF_O = al("sqbf", [128, 1024], BF16, A2 + 18 * KB)
    rows, ROWS_O = al("rows", [1, 4096], F32, A2 + 48 * KB)

    P = [nc.alloc_psum_tensor(f"P{i}", [128, 512], F32).ap() for i in range(8)]

    def PK(i):
        return [("PS", i)]

    xn3 = xn.rearrange("p (k t) -> p k t", k=DC)
    xn23 = xn2.rearrange("p (k t) -> p k t", k=DC) if do_l0 else None
    h13 = h1.rearrange("p (k t) -> p k t", k=32) if do_l0 else None
    h23 = h2.rearrange("p (k t) -> p k t", k=32) if do_l0 else None
    afm3 = attn_fm.rearrange("p (k t) -> p k t", k=H)
    g3 = g_sb.rearrange("p (k t) -> p k t", k=DC)
    int3 = inter.rearrange("p (e k t) -> p e k t", e=NEXP, k=IEXP // 128) if do_l1 else None
    q3 = q.rearrange("p (hh t) -> p hh t", hh=H)

    def dma(queue, out_ap, in_ap, rd, wr, tag=""):
        return sc.op(queue, lambda e, o=out_ap, i=in_ap: e.dma_start(out=o, in_=i),
                     rd=rd, wr=wr, kind="dma", tag=tag)

    def dma_t(out_ap, in_ap, rd, wr, tag=""):
        return sc.op("y", lambda e, o=out_ap, i=in_ap:
                     e.dma_start(out=o, in_=i, transpose=True),
                     rd=rd, wr=wr, kind="dma", tag=tag)

    def mm(out_ap, lhsT, rhs, rd, wr, start=True, stop=True, tag=""):
        return sc.op("t", lambda e, o=out_ap, l=lhsT, r=rhs, st=start, sp=stop:
                     e.matmul(o, l, r, start=st, stop=sp),
                     rd=rd, wr=wr, tag=tag)

    def act(out_ap, in_ap, func, rd, wr, bias=0.0, scale=1.0, accum=None, tag=""):
        return sc.op("s", lambda e, o=out_ap, i=in_ap, f=func, b=bias, sl=scale,
                     a=accum: e.activation(o, i, f, bias=b, scale=sl, accum_out=a),
                     rd=rd, wr=wr, tag=tag, hazard=True)

    def _hz(out_ap):
        return out_ap.free_size() <= 128

    def tt(out_ap, a_ap, b_ap, op, rd, wr, tag=""):
        return sc.op("v", lambda e, o=out_ap, a=a_ap, b=b_ap, p=op:
                     e.tensor_tensor(o, a, b, p), rd=rd, wr=wr, tag=tag,
                     hazard=_hz(out_ap))

    def ts(out_ap, in_ap, scalar, op, rd, wr, tag=""):
        return sc.op("v", lambda e, o=out_ap, i=in_ap, s_=scalar, p=op:
                     e.tensor_scalar(o, i, s_, None, p), rd=rd, wr=wr, tag=tag,
                     hazard=_hz(out_ap))

    def stt(out_ap, in0, scalar, in1, op0, op1, rd, wr, tag=""):
        return sc.op("v", lambda e, o=out_ap, i0=in0, s_=scalar, i1=in1, p0=op0,
                     p1=op1: e.scalar_tensor_tensor(o, i0, s_, i1, p0, p1),
                     rd=rd, wr=wr, tag=tag, hazard=_hz(out_ap))

    def vcopy(out_ap, in_ap, rd, wr, tag=""):
        return sc.op("v", lambda e, o=out_ap, i=in_ap: e.tensor_copy(o, i),
                     rd=rd, wr=wr, tag=tag, hazard=_hz(out_ap))

    def vredmax(out_ap, in_ap, rd, wr, tag="", negate=False):
        return sc.op("v", lambda e, o=out_ap, i=in_ap, n=negate:
                     e.reduce_max(o, i, axis=AX.X, negate=n), rd=rd, wr=wr,
                     tag=tag, hazard=True)

    def vrecip(out_ap, in_ap, rd, wr, tag=""):
        return sc.op("v", lambda e, o=out_ap, i=in_ap: e.reciprocal(o, i),
                     rd=rd, wr=wr, tag=tag, hazard=True)

    CONSTK = [("CONST", 0)]
    consts = [(maskkm, "maskkmd"), (maskqm, "maskqmd"), (onescb, "onescbd"),
              (onesrb, "onesrbd"), (onesrf, "onesrfd"), (idb, "idbd")]
    if do_l0:
        consts += [(ropec, "ropecd"), (ropes, "ropesd"), (rt, "rtd"),
                   (b1r, "b1r"), (b2r, "b2r"), (b3r, "b3r"), (gammar, "gammar")]
    if do_l1:
        consts += [(alibic, "alibicd"), (bqkvr, "bqkvr"), (boutr, "boutr")]
    for ap_, nm in consts:
        dma("g", ap_, w[nm], drk(nm), CONSTK, tag="const")
    if do_l1:
        ts(bqs, bqkvr[:, 0:16], QSCALE, ALU.mult, CONSTK, CONSTK, tag="bqs")

    def emit_stats_chunk(x_ap, x_keys, mc):
        act(xbf, x_ap, AF.Identity, x_keys, sbk(XBF_O, 2048), tag="xbf")
        tt(sqbf, xbf, xbf, ALU.mult, sbk(XBF_O, 2048), sbk(SQBF_O, 2048), tag="sq")
        for th in range(2):
            mm(P[4 + th][0:1, :], onescb, xbf[:, th * 512:(th + 1) * 512],
               sbk(XBF_O, 2048) + CONSTK, PK(4 + th),
               start=(mc == 0), stop=(mc == DC - 1), tag="s1")
            mm(P[6 + th][0:1, :], onescb, sqbf[:, th * 512:(th + 1) * 512],
               sbk(SQBF_O, 2048) + CONSTK, PK(6 + th),
               start=(mc == 0), stop=(mc == DC - 1), tag="s2")

    def emit_stats_from_dram(src, srcname):
        for mc in range(DC):
            xa, xo = xinA[mc % 2]
            dma("y", xa, src[mc * 128:(mc + 1) * 128, :],
                drk(srcname, mc * 128 * T * 4, 128 * T * 4), sbk(xo, 4096), tag="stx")
            emit_stats_chunk(xa, sbk(xo, 4096), mc)

    def emit_rowmath(l0):
        R = [rows[0:1, i * 1024:(i + 1) * 1024] for i in range(4)]
        RK = [sbk(ROWS_O + i * 4096, 4096) for i in range(4)]
        for th in range(2):
            vcopy(R[0][0:1, th * 512:(th + 1) * 512], P[4 + th][0:1, :],
                  PK(4 + th), RK[0], tag="s1cp")
            vcopy(R[1][0:1, th * 512:(th + 1) * 512], P[6 + th][0:1, :],
                  PK(6 + th), RK[1], tag="s2cp")
        ts(R[2], R[0], 1.0 / D, ALU.mult, RK[0], RK[2], tag="mu")
        ts(R[1], R[1], 1.0 / D, ALU.mult, RK[1], RK[1], tag="ms")
        tt(R[0], R[2], R[2], ALU.mult, RK[2], RK[0], tag="musq")
        tt(R[1], R[1], R[0], ALU.subtract, RK[0] + RK[1], RK[1], tag="var")
        ts(R[0], R[1], EPS, ALU.add, RK[1], RK[0], tag="vpe")
        act(R[0], R[0], AF.Ln, RK[0], RK[0], tag="ln")
        act(R[3], R[0], AF.Exp, RK[0], RK[3], scale=-0.5, tag="rstd")
        for th in range(2):
            mm(P[0 + th][:, :], onesrf, R[2][0:1, th * 512:(th + 1) * 512],
               RK[2] + CONSTK, PK(0 + th), tag="bmu")
            mm(P[2 + th][:, :], onesrf, R[3][0:1, th * 512:(th + 1) * 512],
               RK[3] + CONSTK, PK(2 + th), tag="brstd")
        if l0:
            tt(R[0], R[1], R[3], ALU.mult, RK[1] + RK[3], RK[0], tag="v2a")
            tt(R[0], R[0], R[3], ALU.mult, RK[0] + RK[3], RK[0], tag="v2b")
            ts(R[0], R[0], EPS, ALU.add, RK[0], RK[0], tag="v2c")
            act(R[0], R[0], AF.Ln, RK[0], RK[0], tag="ln2")
            act(R[0], R[0], AF.Exp, RK[0], RK[0], scale=-0.5, tag="rstd2")
            for th in range(2):
                mm(P[4 + th][:, :], onesrf, R[0][0:1, th * 512:(th + 1) * 512],
                   RK[0] + CONSTK, PK(4 + th), tag="brstd2")

    def emit_apply(src, srcname, l0):
        for mc in range(DC):
            xa, xo = xinA[mc % 2]
            dma("y", xa, src[mc * 128:(mc + 1) * 128, :],
                drk(srcname, mc * 128 * T * 4, 128 * T * 4), sbk(xo, 4096), tag="apx")
            for th in range(2):
                ta, to = tfA[th]
                tt(ta, xa[:, th * 512:(th + 1) * 512], P[0 + th][:, :], ALU.subtract,
                   sbk(xo, 4096) + PK(0 + th), sbk(to, 2048), tag="sub")
                tt(xn3[:, mc, th * 512:(th + 1) * 512], ta, P[2 + th][:, :], ALU.mult,
                   sbk(to, 2048) + PK(2 + th),
                   sbk(XN_O + mc * 2048 + th * 1024, 1024), tag="xn")
                if l0:
                    stt(xn23[:, mc, th * 512:(th + 1) * 512],
                        xn3[:, mc, th * 512:(th + 1) * 512],
                        gammar[:, mc:mc + 1], P[4 + th][:, :], ALU.mult, ALU.mult,
                        sbk(XN_O + mc * 2048 + th * 1024, 1024) + CONSTK + PK(4 + th),
                        sbk(XN2_O + mc * 2048 + th * 1024, 1024), tag="xn2")

    def xn_keys(k, t0, tn):
        return sbk(XN_O + k * 2048 + t0 * 2, tn * 2)

    def xn2_keys(k, t0, tn):
        return sbk(XN2_O + k * 2048 + t0 * 2, tn * 2)

    def afm_keys(k, t0, tn):
        return sbk(AFM_O + k * 2048 + t0 * 2, tn * 2)

    def wstream(wname, K, m0, m1, stripes, rhs3, rhs_keyfn, evac, mgroup=256,
                pbanks=(0, 1, 2, 3), row0=0):
        KC = K // 128
        rot = [0]
        wr3 = w[wname].rearrange("(k p) m -> p k m", p=128)
        kbase = row0 // 128
        for gi, g0 in enumerate(range(m0, m1, mgroup)):
            sa, so = stripes[gi % 2]
            s3 = sa.rearrange("p (k m) -> p k m", k=KC, m=mgroup)
            dma("g", s3, wr3[:, kbase:kbase + KC, g0:g0 + mgroup],
                drk(wname), sbk(so, 8192), tag="w:" + wname)
            for ml in range(mgroup // 128):
                mc_abs = (g0 + ml * 128) // 128
                for th in range(2):
                    t0, tn = th * 512, 512
                    pi = pbanks[rot[0] % len(pbanks)]
                    rot[0] += 1
                    for k in range(KC):
                        mm(P[pi][:, 0:tn], s3[:, k, ml * 128:(ml + 1) * 128],
                           rhs3[:, k, t0:t0 + tn],
                           sbk(so, 8192) + rhs_keyfn(k, t0, tn), PK(pi),
                           start=(k == 0), stop=(k == KC - 1), tag="mm:" + wname)
                    evac(mc_abs, th, t0, tn, pi)

    def emit_rope(src_ap, src_keys, dst_ap, dst_keys, pb):
        for th in range(2):
            pi = pb[th % len(pb)]
            mm(P[pi][:, :], rt, src_ap[:, th * 512:(th + 1) * 512],
               src_keys + CONSTK, PK(pi), tag="rot")
            ra, ro = rsc[0]
            tt(ra, src_ap[:, th * 512:(th + 1) * 512],
               ropec[:, th * 512:(th + 1) * 512], ALU.mult,
               src_keys + CONSTK, sbk(ro, 2048), tag="rc")
            ra2, ro2 = rsc[1]
            tt(ra2, P[pi][:, :], ropes[:, th * 512:(th + 1) * 512], ALU.mult,
               PK(pi) + CONSTK, sbk(ro2, 2048), tag="rs")
            tt(dst_ap[:, th * 512:(th + 1) * 512], ra, ra2, ALU.add,
               sbk(ro, 2048) + sbk(ro2, 2048), dst_keys, tag="radd")

    def emit_kvq_l0():
        sa, so = wstA[0]
        s3 = sa.rearrange("p (k m) -> p k m", k=DC, m=256)
        dma("g", s3, w["wkv"].rearrange("(k p) m -> p k m", p=128),
            drk("wkv"), sbk(so, 8192), tag="w:wkv")
        for th in range(2):
            pi = th
            for k in range(DC):
                mm(P[pi][:, :], s3[:, k, 0:128], xn23[:, k, th * 512:(th + 1) * 512],
                   sbk(so, 8192) + xn2_keys(k, th * 512, 512), PK(pi),
                   start=(k == 0), stop=(k == DC - 1), tag="mmk0")
            act(ktmp[:, th * 512:(th + 1) * 512], P[pi][:, :], AF.Identity,
                PK(pi), sbk(KT_O + th * 1024, 1024), tag="evk0")
        emit_rope(ktmp, sbk(KT_O, 2048), kvt[0][0], sbk(kvt[0][1], 2048), (2, 3))
        dma("y", ccd["k0i"].ap(), kvt[0][0], sbk(kvt[0][1], 2048),
            drk("k0i"), tag="k0w")
        for th in range(2):
            pi = 2 + th
            for k in range(DC):
                mm(P[pi][:, :], s3[:, k, 128:256], xn23[:, k, th * 512:(th + 1) * 512],
                   sbk(so, 8192) + xn2_keys(k, th * 512, 512), PK(pi),
                   start=(k == 0), stop=(k == DC - 1), tag="mmv0")
            act(kvt[1][0][:, th * 512:(th + 1) * 512], P[pi][:, :], AF.Identity,
                PK(pi), sbk(kvt[1][1] + th * 1024, 1024), tag="evv0")
        dma("y", ccd["v0i"].ap(), kvt[1][0], sbk(kvt[1][1], 2048),
            drk("v0i"), tag="v0w")

    def emit_q_l0():
        KC = DC
        wr3 = w["wq"].rearrange("(k p) m -> p k m", p=128)
        for gi, g0 in enumerate(range(0, D, 256)):
            sa, so = wstA[gi % 2]
            s3 = sa.rearrange("p (k m) -> p k m", k=KC, m=256)
            dma("g", s3, wr3[:, :, g0:g0 + 256], drk("wq"), sbk(so, 8192), tag="w:wq")
            for ml in range(2):
                h = (g0 + ml * 128) // 128
                for th in range(2):
                    pi = th
                    for k in range(KC):
                        mm(P[pi][:, :], s3[:, k, ml * 128:(ml + 1) * 128],
                           xn23[:, k, th * 512:(th + 1) * 512],
                           sbk(so, 8192) + xn2_keys(k, th * 512, 512), PK(pi),
                           start=(k == 0), stop=(k == KC - 1), tag="mmq0")
                    act(qtmp, P[pi][:, :], AF.Identity, PK(pi),
                        sbk(QT_O, 1024), scale=QSCALE, tag="evq0")
                    pi2 = 2 + th
                    mm(P[pi2][:, :], rt, qtmp, sbk(QT_O, 1024) + CONSTK, PK(pi2),
                       tag="rotq")
                    ra, ro = rsc[0]
                    tt(ra, qtmp, ropec[:, th * 512:(th + 1) * 512], ALU.mult,
                       sbk(QT_O, 1024) + CONSTK, sbk(ro, 2048), tag="rcq")
                    ra2, ro2 = rsc[1]
                    tt(ra2, P[pi2][:, :], ropes[:, th * 512:(th + 1) * 512], ALU.mult,
                       PK(pi2) + CONSTK, sbk(ro2, 2048), tag="rsq")
                    tt(q3[:, h, th * 512:(th + 1) * 512], ra, ra2, ALU.add,
                       sbk(ro, 2048) + sbk(ro2, 2048),
                       sbk(Q_O + h * 2048 + th * 1024, 1024), tag="raddq")

    def emit_kvq_l1():
        KC = DC
        wr3 = w["wqkvt"].rearrange("(k p) m -> p k m", p=128)
        for gi, g0 in enumerate(range(0, 3 * D, 256)):
            sa, so = wstA[gi % 2]
            s3 = sa.rearrange("p (k m) -> p k m", k=KC, m=256)
            dma("g", s3, wr3[:, :, g0:g0 + 256], drk("wqkvt"), sbk(so, 8192),
                tag="w:wqkvt")
            for ml in range(2):
                mc = (g0 + ml * 128) // 128
                kind = mc // 16
                mloc = mc % 16
                kv, kvo = kvt[mc % 2]
                for th in range(2):
                    pi = (mc * 2 + th) % 4
                    for k in range(KC):
                        mm(P[pi][:, :], s3[:, k, ml * 128:(ml + 1) * 128],
                           xn3[:, k, th * 512:(th + 1) * 512],
                           sbk(so, 8192) + xn_keys(k, th * 512, 512), PK(pi),
                           start=(k == 0), stop=(k == KC - 1), tag="mmqkv")
                    if kind == 0:
                        act(q3[:, mloc, th * 512:(th + 1) * 512], P[pi][:, :],
                            AF.Identity, PK(pi) + CONSTK,
                            sbk(Q_O + mloc * 2048 + th * 1024, 1024),
                            bias=bqs[:, mloc:mloc + 1], scale=QSCALE, tag="evq1")
                    else:
                        act(kv[:, th * 512:(th + 1) * 512], P[pi][:, :], AF.Identity,
                            PK(pi) + CONSTK, sbk(kvo + th * 1024, 1024),
                            bias=bqkvr[:, 16 * kind + mloc:16 * kind + mloc + 1],
                            tag="evkv1")
                if kind == 1:
                    dma("y", ccd["k1i"].ap()[mloc * 128:(mloc + 1) * 128, :], kv,
                        sbk(kvo, 2048), drk("k1i", mloc * 128 * T * 2, 128 * T * 2),
                        tag="k1w")
                elif kind == 2:
                    dma("y", ccd["v1i"].ap()[mloc * 128:(mloc + 1) * 128, :], kv,
                        sbk(kvo, 2048), drk("v1i", mloc * 128 * T * 2, 128 * T * 2),
                        tag="v1w")

    def emit_collective(ki, ko, vi, vo):
        for inm, onm in ((ki, ko), (vi, vo)):
            sc.op("g", lambda e, a=ccd[inm], b=ccd[onm]: e.collective_compute(
                "AllGather", ALU.bypass,
                replica_groups=[[0, 4], [1, 5], [2, 6], [3, 7]],
                ins=[a.ap().opt()], outs=[b.ap().opt()]),
                rd=drk(inm), wr=drk(onm), kind="cc", tag="cc:" + inm)

    def emit_ffn_l0():
        def ev_h1(mc, th, t0, tn, pi):
            act(h13[:, mc, t0:t0 + tn], P[pi][:, 0:tn], AF.Relu,
                PK(pi) + CONSTK, sbk(H1_O + mc * 2048 + t0 * 2, tn * 2),
                bias=b1r[:, mc:mc + 1], tag="evh1")
        wstream("w1t", D, 0, 4096, wstB, xn3, xn_keys, ev_h1,
                pbanks=(0, 1, 2, 3, 4, 5))

        def h1k(k, t0, tn):
            return sbk(H1_O + k * 2048 + t0 * 2, tn * 2)

        def ev_h2(mc, th, t0, tn, pi):
            act(h23[:, mc, t0:t0 + tn], P[pi][:, 0:tn], AF.Relu,
                PK(pi) + CONSTK, sbk(H2_O + mc * 2048 + t0 * 2, tn * 2),
                bias=b2r[:, mc:mc + 1], tag="evh2")
        wstream("w2t", 4096, 0, 4096, wstC, h13, h1k, ev_h2, mgroup=128,
                pbanks=(0, 1, 2, 3, 4, 5))

        def h2k(k, t0, tn):
            return sbk(H2_O + k * 2048 + t0 * 2, tn * 2)

        def ev_g(mc, th, t0, tn, pi):
            act(g3[:, mc, t0:t0 + tn], P[pi][:, 0:tn], AF.Silu,
                PK(pi) + CONSTK, sbk(G_O + mc * 4096 + t0 * 4, tn * 4),
                bias=b3r[:, mc:mc + 1], tag="evg")
        wstream("w3t", 4096, 0, D, wstC, h23, h2k, ev_g, mgroup=128,
                pbanks=(0, 1, 2, 3, 4, 5))

    def emit_ffn_l1():
        IC = IEXP // 128
        wg3 = w["gup"].rearrange("(k p) m -> p k m", p=128)
        for e_ in range(NEXP):
            for gi, j0 in enumerate(range(0, IEXP, 256)):
                sa, so = gupst[2 * (gi % 2)] if FFN_NEWBUF else wstB[0]
                sau, sou = gupst[2 * (gi % 2) + 1] if FFN_NEWBUF else wstB[1]
                sg3 = sa.rearrange("p (k m) -> p k m", k=DC, m=256)
                su3 = sau.rearrange("p (k m) -> p k m", k=DC, m=256)
                dma("g", sg3, wg3[:, e_ * DC:(e_ + 1) * DC, j0:j0 + 256],
                    drk("gup"), sbk(so, 8192), tag="w:gupg")
                dma("g", su3, wg3[:, e_ * DC:(e_ + 1) * DC,
                               IEXP + j0:IEXP + j0 + 256],
                    drk("gup"), sbk(sou, 8192), tag="w:gupu")
                for jl in range(2):
                    jc = (j0 + jl * 128) // 128
                    pg, pu = (((0, 1), (2, 3), (4, 5), (6, 7))[jc % 4]
                              if FFN_NEW else ((0, 1), (2, 3))[jc % 2])
                    for k in range(DC):
                        mm(P[pg][:, 0:256], sg3[:, k, jl * 128:(jl + 1) * 128],
                           xn3[:, k, e_::4],
                           sbk(so, 8192) + xn_keys(k, 0, 1024), PK(pg),
                           start=(k == 0), stop=(k == DC - 1), tag="mmgate")
                    for k in range(DC):
                        mm(P[pu][:, 0:256], su3[:, k, jl * 128:(jl + 1) * 128],
                           xn3[:, k, e_::4],
                           sbk(sou, 8192) + xn_keys(k, 0, 1024), PK(pu),
                           start=(k == 0), stop=(k == DC - 1), tag="mmup")
                    pb, pbo = psb[0]
                    act(pb[:, 0:256], P[pg][:, 0:256], AF.Silu, PK(pg),
                        sbk(pbo, 512), tag="evgate")
                    tt(int3[:, e_, jc, :], pb[:, 0:256], P[pu][:, 0:256], ALU.mult,
                       sbk(pbo, 512) + PK(pu),
                       sbk(INTER_O + e_ * 8192 + jc * 512, 512), tag="inter")
        wd3 = w["dwn"].rearrange("(k p) m -> p k m", p=128)
        for e_ in range(NEXP):
            for gi, m0 in enumerate(range(0, D, 256)):
                sa, so = wstB[gi % 2]
                s3 = sa.rearrange("p (k m) -> p k m", k=IC, m=256)
                dma("g", s3, wd3[:, e_ * IC:(e_ + 1) * IC, m0:m0 + 256],
                    drk("dwn"), sbk(so, 8192), tag="w:dwn")
                for ml in range(2):
                    mc = (m0 + ml * 128) // 128
                    pi = mc % 4
                    for k in range(IC):
                        mm(P[pi][:, 0:256], s3[:, k, ml * 128:(ml + 1) * 128],
                           int3[:, e_, k, :],
                           sbk(so, 8192) + sbk(INTER_O + e_ * 8192 + k * 512, 512),
                           PK(pi), start=(k == 0), stop=(k == IC - 1), tag="mmdwn")
                    act(g3[:, mc, e_::4], P[pi][:, 0:256], AF.Silu,
                        PK(pi), sbk(G_O + mc * 4096, 4096), tag="evg1")

    def emit_attention(layer):
        l1 = layer == 1
        kon = "k1o" if l1 else "k0o"
        von = "v1o" if l1 else "v0o"
        ko = ccd[kon].ap()
        vo = ccd[von].ap()
        NEGROWK = sbk(NEGROW_O, 2048)
        INVK = sbk(INVROW_O, 4096)

        def load_kv(h, kbuf, kko, vbuf, vko):
            for r in range(2):
                src = (ko[r * D + h * 128:r * D + h * 128 + 128, :] if l1
                       else ko[r * 128:(r + 1) * 128, :])
                dma("y", kbuf[:, r * 1024:(r + 1) * 1024], src,
                    drk(kon), sbk(kko + r * 2048, 2048), tag="kld")
            for c in range(16):
                s_k, r = c // 2, c % 2
                src = (vo[r * D + h * 128:r * D + h * 128 + 128,
                          s_k * 128:(s_k + 1) * 128] if l1
                       else vo[r * 128:(r + 1) * 128, s_k * 128:(s_k + 1) * 128])
                dma_t(vbuf[:, c * 128:(c + 1) * 128], src,
                      drk(von), sbk(vko + c * 256, 256), tag="vld")

        if not l1:
            load_kv(0, kb_[0][0], kb_[0][1], vb_[0][0], vb_[0][1])

        for h in range(H):
            if l1:
                kbuf, kko = kb_[h % 2]
                vbuf, vko = vb_[h % 2]
                load_kv(h, kbuf, kko, vbuf, vko)
                ab, abo = arowb[h % 2]
                dma("y", ab, w["arowd"][h:h + 1, :], drk("arowd"),
                    sbk(abo, 4096), tag="arow")
            else:
                kbuf, kko = kb_[0]
                vbuf, vko = vb_[0]
            KBK = sbk(kko, 4096)
            VBK = sbk(vko, 4096)

            qrot = [0]
            for s in range(8):
                qsl = q3[:, h, s * 128:(s + 1) * 128]
                QSK = sbk(Q_O + h * 2048 + s * 256, 256)
                np_ = 0
                for r in range(2):
                    cap = (s + 1) * 128
                    for k0 in range(0, cap, 512):
                        kn = min(512, cap - k0)
                        pi = (0, 1)[qrot[0] % 2]
                        qrot[0] += 1
                        mm(P[pi][:, 0:kn], qsl,
                           kbuf[:, r * 1024 + k0:r * 1024 + k0 + kn],
                           QSK + KBK, PK(pi), start=True, stop=not l1, tag="qmj")
                        if l1:
                            mm(P[pi][:, 0:kn], onesrb,
                               ab[0:1, r * 1024 + k0:r * 1024 + k0 + kn],
                               CONSTK + sbk(abo, 4096), PK(pi),
                               start=False, stop=True, tag="qmja")
                        if k0 + kn == cap:
                            tt(P[pi][:, kn - 128:kn], P[pi][:, kn - 128:kn],
                               maskqm[:, r * 128:(r + 1) * 128], ALU.add,
                               PK(pi) + CONSTK, PK(pi), tag="qmmask")
                        col = s * 4 + np_
                        np_ += 1
                        vredmax(pmax[:, col:col + 1], P[pi][:, 0:kn], PK(pi),
                                sbk(PMAX_O, 128), tag="qmred")
                vredmax(negbs[:, s:s + 1], pmax[:, s * 4:s * 4 + np_],
                        sbk(PMAX_O, 128), sbk(NEGBS_O, 16), tag="qmneg",
                        negate=True)
                if DEBUG_DUMP and l1 and h == 0 and s == 0:
                    for rr_ in range(2):
                        dq = nc.dram_tensor(f"dbg_qmj{rr_}", [128, 128], F32,
                                            kind="ExternalOutput").ap()
                        scr = nc.alloc_sbuf_tensor_at(
                            f"dbg_scr{rr_}", [128, 128], F32,
                            offset=A1 + 24 * KB + rr_ * 512).ap()
                        act(scr, P[rr_][:, 0:128], AF.Identity, PK(rr_),
                            sbk(A1 + 24 * KB + rr_ * 512, 512), tag="dbgev")
                        dma("y", dq, scr, sbk(A1 + 24 * KB + rr_ * 512, 512),
                            drk(f"dbg_qmj{rr_}"), tag="dbgq")

            for s in range(8):
                mm(P[6 + s // 4][0:1, (s % 4) * 128:(s % 4) * 128 + 128],
                   negbs[:, s:s + 1], idb, sbk(NEGBS_O, 16) + CONSTK,
                   PK(6 + s // 4), start=True, stop=True, tag="negt")
            for hf in range(2):
                act(negrow[0:1, hf * 512:(hf + 1) * 512], P[6 + hf][0:1, 0:512],
                    AF.Identity, PK(6 + hf), NEGROWK, tag="negev")

            if DEBUG_DUMP and l1 and h == 0:
                dnr = nc.dram_tensor("dbg_negrow", [1, 1024], BF16,
                                     kind="ExternalOutput").ap()
                dma("y", dnr, negrow, NEGROWK, drk("dbg_negrow"), tag="dbgnr")

            krot = [0]
            for c in range(16):
                s_k, r = c // 2, c % 2
                base = r * 1024 + s_k * 128
                q0full = s_k * 128
                pieces = []
                p0 = q0full
                while p0 < 1024:
                    pn = min(512 - p0 % 512, 1024 - p0)
                    pieces.append((p0, pn))
                    p0 += pn
                for (q0, pn) in pieces:
                    pi = (0, 1, 6, 7)[krot[0] % KM_NBANK]
                    krot[0] += 1
                    mm(P[pi][:, 0:pn], kbuf[:, base:base + 128],
                       q3[:, h, q0:q0 + pn],
                       KBK + sbk(Q_O + h * 2048 + q0 * 2, pn * 2), PK(pi),
                       start=True, stop=False, tag="kmj")
                    mm(P[pi][:, 0:pn], onesrb, negrow[0:1, q0:q0 + pn],
                       CONSTK + NEGROWK, PK(pi), start=False, stop=True, tag="kmjm")
                    if q0 == q0full:
                        tt(P[pi][:, 0:128], P[pi][:, 0:128],
                           maskkm[:, r * 128:(r + 1) * 128], ALU.add,
                           PK(pi) + CONSTK, PK(pi), tag="kmmask")
                    pb, pbo = psb[(krot[0] - 1) % KM_NBANK]
                    bias = alibic[:, h * 16 + c:h * 16 + c + 1] if l1 else 0.0
                    act(pb[:, 0:pn], P[pi][:, 0:pn], AF.Exp, PK(pi) + CONSTK,
                        sbk(pbo, 1024), bias=bias, tag="exp")
                    half = q0 // 512
                    qq = q0 % 512
                    mm(P[4 + half][0:1, qq:qq + pn], onescb, pb[:, 0:pn],
                       CONSTK + sbk(pbo, 1024), PK(4 + half),
                       start=(c == 0), stop=(c == 15), tag="rsum")
                    mm(P[2 + half][:, qq:qq + pn], vbuf[:, c * 128:(c + 1) * 128],
                       pb[:, 0:pn], VBK + sbk(pbo, 1024), PK(2 + half),
                       start=(c == 0), stop=(c == 15), tag="av")
            for half in range(2):
                vrecip(invrow[0:1, half * 512:(half + 1) * 512], P[4 + half][0:1, :],
                       PK(4 + half), INVK, tag="recip")
                pi = half
                mm(P[pi][:, :], onesrf, invrow[0:1, half * 512:(half + 1) * 512],
                   CONSTK + INVK, PK(pi), tag="binv")
                ab_, abo_ = avbf[half]
                act(ab_, P[2 + half][:, :], AF.Identity, PK(2 + half),
                    sbk(abo_, 1024), tag="avev")
                tt(afm3[:, h, half * 512:(half + 1) * 512], ab_, P[pi][:, :],
                   ALU.mult, sbk(abo_, 1024) + PK(pi),
                   sbk(AFM_O + h * 2048 + half * 1024, 1024), tag="aout")
            if DEBUG_DUMP and l1 and h == 0:
                dir_ = nc.dram_tensor("dbg_invrow", [1, 1024], F32,
                                      kind="ExternalOutput").ap()
                dma("y", dir_, invrow, INVK, drk("dbg_invrow"), tag="dbgir")

    def emit_combine(layer, x_src, srcname, x1_dst, dstname, fuse_stats):
        l1 = layer == 1
        wname = "woutt" if l1 else "wo"
        KC = DC
        wr3 = w[wname].rearrange("(k p) m -> p k m", p=128)
        for gi, g0 in enumerate(range(0, D, 256)):
            sa, so = wstC[gi % 2]
            s3 = sa.rearrange("p (k m) -> p k m", k=KC, m=256)
            dma("g", s3, wr3[:, :, g0:g0 + 256], drk(wname), sbk(so, 8192),
                tag="w:" + wname)
            for ml in range(2):
                mc = (g0 + ml * 128) // 128
                xa, xo = xinB[mc % 2]
                dma("y", xa, x_src[mc * 128:(mc + 1) * 128, :],
                    drk(srcname, mc * 128 * T * 4, 128 * T * 4), sbk(xo, 4096),
                    tag="cbx")
                xb, xbo = x1b[mc % 2]
                for th in range(2):
                    pi = (mc * 2 + th) % 4
                    for k in range(KC):
                        mm(P[pi][:, :], s3[:, k, ml * 128:(ml + 1) * 128],
                           afm3[:, k, th * 512:(th + 1) * 512],
                           sbk(so, 8192) + afm_keys(k, th * 512, 512), PK(pi),
                           start=(k == 0), stop=(k == KC - 1), tag="mmwo")
                    ta, to = tfB[th]
                    bias = boutr[:, mc:mc + 1] if l1 else 0.0
                    stt(ta, P[pi][:, :], bias, g3[:, mc, th * 512:(th + 1) * 512],
                        ALU.add, ALU.mult,
                        PK(pi) + CONSTK + sbk(G_O + mc * 4096 + th * 2048, 2048),
                        sbk(to, 2048), tag="sg")
                    tt(xb[:, th * 512:(th + 1) * 512], ta,
                       xa[:, th * 512:(th + 1) * 512], ALU.add,
                       sbk(to, 2048) + sbk(xo, 4096),
                       sbk(xbo + th * 2048, 2048), tag="x1")
                dma("y", x1_dst[mc * 128:(mc + 1) * 128, :], xb, sbk(xbo, 4096),
                    drk(dstname, mc * 128 * T * 4, 128 * T * 4), tag="x1w")
                if fuse_stats:
                    emit_stats_chunk(xb, sbk(xbo, 4096), mc)

    if do_l0:
        sc.phase("L0stats"); emit_stats_from_dram(x_in, "x")
        sc.phase("L0rows"); emit_rowmath(l0=True)
        sc.phase("L0apply"); emit_apply(x_in, "x", l0=True)
        sc.phase("L0kv"); emit_kvq_l0()
        sc.phase("L0cc"); emit_collective("k0i", "k0o", "v0i", "v0o")
        sc.phase("L0q"); emit_q_l0()
        sc.phase("L0ffn"); emit_ffn_l0()
        sc.phase("L0attn"); emit_attention(0)
        sc.phase("L0comb"); emit_combine(0, x_in, "x", xmid, "xmid", fuse_stats=(mode == "fused"))
    if do_l1:
        if mode == "l1":
            emit_stats_from_dram(xmid, "xmid")
        sc.phase("L1rows"); emit_rowmath(l0=False)
        sc.phase("L1apply"); emit_apply(xmid, "xmid", l0=False)
        sc.phase("L1kv"); emit_kvq_l1()
        sc.phase("L1cc"); emit_collective("k1i", "k1o", "v1i", "v1o")
        sc.phase("L1ffn"); emit_ffn_l1()
        sc.phase("L1attn"); emit_attention(1)
        sc.phase("L1comb"); emit_combine(1, xmid, "xmid", out, "out", fuse_stats=False)

    if DEBUG_DUMP:
        dumps = [("xn", xn, BF16, DC * 1024, XN_O),
                 ("q", q, BF16, H * 1024, Q_O),
                 ("attn_fm", attn_fm, BF16, H * 1024, AFM_O),
                 ("g", g_sb, F32, DC * 1024, G_O)]
        for nm, ap_, dt, ncols, o_ in dumps:
            dd = nc.dram_tensor("dbg_" + nm, [128, ncols], dt,
                                kind="ExternalOutput").ap()
            dma("y", dd, ap_, sbk(o_, ncols * (2 if dt == BF16 else 4)),
                drk("dbg_" + nm), tag="dbg")

    from contextlib import ExitStack
    with ExitStack() as stack:
        sems = {}
        spaces = ["t", "v", "s", "g", "y", "cc"]
        for q in ("gq", "yq", "sq"):
            spaces += [f"{q}{i}" for i in range(NQSEM)]
        for sp in spaces:
            sems[sp] = stack.enter_context(nc.semaphore(sp))
        block = stack.enter_context(nc.Block())
        body = sc.make_bodies(sems)
        block.tensor(body("t", nc))
        block.vector(body("v", nc))
        block.scalar(body("s", nc))
        block.gpsimd(body("g", nc))
        block.sync(body("y", nc))
    nc._sched = sc
    return nc


def _prep_shared(inp):
    f = np.float32
    sh = {}
    sh["wq"] = np.ascontiguousarray(inp["L0_attn_Wq"], f)
    sh["wkv"] = np.ascontiguousarray(inp["L0_attn_Wkv"], f)
    sh["wo"] = np.ascontiguousarray(inp["L0_attn_Wo"], f)
    sh["w1t"] = np.ascontiguousarray(np.asarray(inp["L0_ffn_W1"], f).T)
    sh["w2t"] = np.ascontiguousarray(np.asarray(inp["L0_ffn_W2"], f).T)
    sh["w3t"] = np.ascontiguousarray(np.asarray(inp["L0_ffn_W3"], f).T)
    sh["b1r"] = np.ascontiguousarray(np.asarray(inp["L0_ffn_b1"], f).reshape(32, 128).T)
    sh["b2r"] = np.ascontiguousarray(np.asarray(inp["L0_ffn_b2"], f).reshape(32, 128).T)
    sh["b3r"] = np.ascontiguousarray(np.asarray(inp["L0_ffn_b3"], f).reshape(16, 128).T)
    sh["gammar"] = np.ascontiguousarray(np.asarray(inp["L0_attn_gamma"], f).reshape(16, 128).T)
    sh["wqkvt"] = np.ascontiguousarray(np.asarray(inp["L1_attn_Wqkv"], f).T)
    sh["woutt"] = np.ascontiguousarray(np.asarray(inp["L1_attn_Wout"], f).T)
    sh["gup"] = np.ascontiguousarray(np.asarray(inp["L1_ffn_gate_up_proj"], f)
                                     .reshape(NEXP * D, 2 * IEXP))
    sh["dwn"] = np.ascontiguousarray(np.asarray(inp["L1_ffn_down_proj"], f)
                                     .reshape(NEXP * IEXP, D))
    sh["bqkvr"] = np.ascontiguousarray(np.asarray(inp["L1_attn_b_qkv"], f).reshape(48, 128).T)
    sh["boutr"] = np.ascontiguousarray(np.asarray(inp["L1_attn_b_out"], f).reshape(16, 128).T)

    rtm = np.zeros((128, 128), f)
    for jj in range(64):
        rtm[64 + jj, jj] = -1.0
    for jj in range(64, 128):
        rtm[jj - 64, jj] = 1.0
    sh["rtd"] = rtm.astype(BF)

    sh["onescbd"] = np.ones((128, 1), BF)
    sh["onesrbd"] = np.ones((1, 128), BF)
    sh["onesrfd"] = np.ones((1, 128), f)
    sh["idbd"] = np.eye(128, dtype=BF)

    hh = np.arange(H)
    slopes = 2.0 ** (-(hh + 1) * 0.5)
    cpos = np.zeros((128, H * 16), f)
    for h in range(H):
        for c in range(16):
            s_k, r = c // 2, c % 2
            pos = (2 * s_k + r) * 128 + np.arange(128)
            cpos[:, h * 16 + c] = ((pos - (SEQ - 1)) * slopes[h]).astype(f)
    sh["alibicd"] = cpos
    ar = np.zeros((H, SEQ), np.float64)
    col = np.zeros(SEQ, np.int64)
    for r in range(2):
        for s in range(8):
            col[r * 1024 + s * 128:r * 1024 + (s + 1) * 128] = \
                (2 * s + r) * 128 + np.arange(128)
    for h in range(H):
        ar[h] = (col - (SEQ - 1)) * slopes[h]
    sh["arowd"] = ar.astype(BF)
    return sh


def _prep_core(inp, c):
    f = np.float32
    b, par = c % 4, c // 4
    tiles = [2 * s + par for s in range(8)]
    xs = np.asarray(inp["L0_attn_x"][b], f)
    xcore = np.ascontiguousarray(
        np.concatenate([xs[t * 128:(t + 1) * 128] for t in tiles], 0).T)
    d = {"x": xcore}
    pos = np.concatenate([np.arange(t * 128, (t + 1) * 128) for t in tiles])
    inv = 1.0 / (10000.0 ** (np.arange(0, DH, 2) / DH))
    fr = pos[:, None] * inv[None, :]
    fr = np.concatenate([fr, fr], 1)
    d["ropecd"] = np.ascontiguousarray(np.cos(fr).T.astype(f))
    d["ropesd"] = np.ascontiguousarray(np.sin(fr).T.astype(f))
    i_ = np.arange(128)[:, None]
    j_ = np.arange(128)[None, :]
    tri = np.where(i_ <= j_, 0.0, NEG).astype(f)
    zer = np.zeros((128, 128), f)
    full = np.full((128, 128), NEG, f)
    if par == 0:
        km = np.concatenate([tri, full], 1)
        qm = np.concatenate([np.ascontiguousarray(tri.T), full], 1)
    else:
        km = np.concatenate([zer, tri], 1)
        qm = np.concatenate([zer, np.ascontiguousarray(tri.T)], 1)
    d["maskkmd"] = np.ascontiguousarray(km)
    d["maskqmd"] = np.ascontiguousarray(qm)
    return d, b, tiles


def kernel(**inputs):
    shared = _prep_shared(inputs)
    per_core, metas = [], []
    for c in range(8):
        d, b, tiles = _prep_core(inputs, c)
        d.update(shared)
        per_core.append(d)
        metas.append((b, tiles))

    outfull = np.zeros((4, SEQ, D), np.float32)

    def scatter(res, key):
        for c in range(8):
            o = np.asarray(res.results[c][key]).T
            b, tiles = metas[c]
            for si, t in enumerate(tiles):
                outfull[b, t * 128:(t + 1) * 128, :] = o[si * 128:(si + 1) * 128]

    if MODE == "fused":
        nc = build_program("fused")
        res = run_bass_kernel_spmd(nc, per_core, core_ids=list(range(8)))
        scatter(res, "out")
    else:
        nc0 = build_program("l0")
        res0 = run_bass_kernel_spmd(nc0, per_core, core_ids=list(range(8)))
        nc1 = build_program("l1")
        maps1 = []
        for c in range(8):
            d = dict(per_core[c])
            d["xmid"] = np.ascontiguousarray(res0.results[c]["xmid"])
            maps1.append(d)
        res1 = run_bass_kernel_spmd(nc1, maps1, core_ids=list(range(8)))
        scatter(res1, "out")
    return outfull


# version 03
# speedup vs baseline: 1.0949x; 1.0311x over previous
import numpy as np
import ml_dtypes
import concourse.bass as bass
import concourse.mybir as mybir
from concourse.bass_utils import run_bass_kernel_spmd

F32 = mybir.dt.float32
BF16 = mybir.dt.bfloat16
AF = mybir.ActivationFunctionType
ALU = mybir.AluOpType
AX = mybir.AxisListType
BF = ml_dtypes.bfloat16

D = 2048
DC = 16
T = 1024
H = 16
DH = 128
SEQ = 2048
EPS = 1e-5
QSCALE = DH ** -0.5
NEXP = 4
IEXP = 2048
NEG = -1.0e30

MODE = "fused"
KM_NBANK = 4
NQSEM = 8
FFN_NEW = True
FFN_NEWBUF = True
DEBUG_DUMP = False

KB = 1024
PAGE = 2048


class Op:
    __slots__ = ("eng", "kind", "emit", "space", "inc_idx", "deps",
                 "snapshot", "waits", "tag", "hazard", "phase")

    def __init__(self, eng, kind, emit, tag="", hazard=False, phase=""):
        self.eng, self.kind, self.emit, self.tag = eng, kind, emit, tag
        self.hazard = hazard
        self.phase = phase
        self.space = None
        self.inc_idx = None
        self.deps = []
        self.snapshot = None
        self.waits = []


def _space_of(op):
    if op.kind == "cc":
        return "cc"
    if op.kind == "dma":
        return {"g": "gq", "y": "yq", "s": "sq"}[op.eng]
    return op.eng


class Sched:
    def __init__(self):
        self.ops = []
        self.writer = {}
        self.readers = {}
        self.cur_phase = ""

    def phase(self, name):
        self.cur_phase = name

    def op(self, eng, emit, rd=(), wr=(), kind="compute", tag="", hazard=False):
        o = Op(eng, kind, emit, tag, hazard, self.cur_phase)
        deps = []
        for k in rd:
            w = self.writer.get(k)
            if w is not None:
                deps.append(w)
        for k in wr:
            w = self.writer.get(k)
            if w is not None:
                deps.append(w)
            deps.extend(self.readers.get(k, ()))
        pruned, seen = [], set()
        for d in deps:
            if d is o or id(d) in seen:
                continue
            seen.add(id(d))
            if d.eng == o.eng:
                if d.kind == "compute" and not d.hazard:
                    continue
                if d.kind == "dma" and o.kind == "dma":
                    continue
            pruned.append(d)
        o.deps = pruned
        for k in rd:
            self.readers.setdefault(k, []).append(o)
        for k in wr:
            self.writer[k] = o
            self.readers[k] = []
        self.ops.append(o)
        return o

    def finalize(self):
        needed = set()
        for o in self.ops:
            for d in o.deps:
                needed.add(id(d))
        qrr = {}
        for o in self.ops:
            sp = _space_of(o)
            if o.kind == "dma":
                i = qrr.get(sp, 0)
                qrr[sp] = i + 1
                sp = f"{sp}{i % NQSEM}"
            o.space = sp
        counters = {}
        for o in self.ops:
            if id(o) in needed or o.kind in ("dma", "cc"):
                counters[o.space] = counters.get(o.space, 0) + 1
                o.inc_idx = counters[o.space]
        self.space_totals = counters
        know = {e: {} for e in ("t", "v", "s", "g", "y")}
        for o in self.ops:
            k = know[o.eng]
            merged = None
            for d in o.deps:
                cur = merged if merged is not None else k
                if cur.get(d.space, 0) >= d.inc_idx:
                    continue
                if merged is None:
                    merged = dict(k)
                if d.snapshot:
                    for sp, v in d.snapshot.items():
                        if merged.get(sp, 0) < v:
                            merged[sp] = v
                if merged.get(d.space, 0) < d.inc_idx:
                    merged[d.space] = d.inc_idx
            if merged is not None:
                for sp, v in merged.items():
                    if k.get(sp, 0) < v:
                        o.waits.append((sp, v))
                know[o.eng] = merged
                k = merged
            if o.inc_idx is not None:
                snap = dict(k)
                snap[o.space] = max(snap.get(o.space, 0), o.inc_idx)
                o.snapshot = snap
            else:
                o.snapshot = k

    def make_bodies(self, sems):
        self.finalize()
        streams = {e: [o for o in self.ops if o.eng == e]
                   for e in ("t", "v", "s", "g", "y")}
        step = {sp: (16 if sp[0:2] in ("gq", "yq", "sq") else 1) for sp in sems}

        def body(engname, nc=None):
            def run(eng):
                cur = [None]

                def emit_one(o):
                    for sp, v in o.waits:
                        eng.wait_ge(sems[sp], v * step[sp])
                    inst = o.emit(eng)
                    if o.inc_idx is not None:
                        inst.then_inc(sems[o.space], step[o.space])
                for o in streams[engname]:
                    if nc is not None and o.phase != cur[0]:
                        cur[0] = o.phase
                    if nc is not None and o.phase:
                        with nc.named_scope(o.phase):
                            emit_one(o)
                    else:
                        emit_one(o)
                for sp, tot in self.space_totals.items():
                    eng.wait_ge(sems[sp], tot * step[sp])
            return run
        return body


def sbk(off, size):
    return [("SB", p) for p in range(off // PAGE, (off + size - 1) // PAGE + 1)]


def drk(name, off=0, size=1 << 30):
    return [("D" + name, p) for p in range(off // 65536, (off + size - 1) // 65536 + 1)]


def build_program(mode):
    nc = bass.Bass("TRN2", target_bir_lowering=False)
    sc = Sched()
    do_l0 = mode in ("l0", "fused")
    do_l1 = mode in ("l1", "fused")

    def din(name, shape, dt=F32):
        return nc.dram_tensor(name, shape, dt, kind="ExternalInput").ap()

    x_in = din("x", [D, T]) if do_l0 else None
    if mode == "l0":
        xmid = nc.dram_tensor("xmid", [D, T], F32, kind="ExternalOutput").ap()
    elif mode == "l1":
        xmid = din("xmid", [D, T])
    else:
        xmid = nc.dram_tensor("xmid", [D, T], F32, kind="Internal").ap()
    out = (nc.dram_tensor("out", [D, T], F32, kind="ExternalOutput").ap()
           if do_l1 else None)

    w = {}
    if do_l0:
        for n, s in [("wq", [D, D]), ("wkv", [D, 256]), ("wo", [D, D]),
                     ("w1t", [D, 4096]), ("w2t", [4096, 4096]), ("w3t", [4096, D]),
                     ("b1r", [128, 32]), ("b2r", [128, 32]), ("b3r", [128, 16]),
                     ("gammar", [128, 16]), ("ropecd", [128, T]), ("ropesd", [128, T])]:
            w[n] = din(n, s)
        w["rtd"] = din("rtd", [128, 128], BF16)
    if do_l1:
        for n, s in [("wqkvt", [D, 3 * D]), ("woutt", [D, D]),
                     ("gup", [NEXP * D, 2 * IEXP]), ("dwn", [NEXP * IEXP, D]),
                     ("bqkvr", [128, 48]), ("boutr", [128, 16]),
                     ("alibicd", [128, 256])]:
            w[n] = din(n, s)
        w["arowd"] = din("arowd", [H, SEQ], BF16)
    w["maskkmd"] = din("maskkmd", [128, 256])
    w["maskqmd"] = din("maskqmd", [128, 256])
    w["onescbd"] = din("onescbd", [128, 1], BF16)
    w["onesrbd"] = din("onesrbd", [1, 128], BF16)
    w["onesrfd"] = din("onesrfd", [1, 128])
    w["idbd"] = din("idbd", [128, 128], BF16)

    ccd = {}
    if do_l0:
        ccd["k0i"] = nc.dram_tensor("k0i", [128, T], BF16, kind="Internal")
        ccd["k0o"] = nc.dram_tensor("k0o", [256, T], BF16, kind="Internal")
        ccd["v0i"] = nc.dram_tensor("v0i", [128, T], BF16, kind="Internal")
        ccd["v0o"] = nc.dram_tensor("v0o", [256, T], BF16, kind="Internal")
    if do_l1:
        ccd["k1i"] = nc.dram_tensor("k1i", [D, T], BF16, kind="Internal")
        ccd["k1o"] = nc.dram_tensor("k1o", [2 * D, T], BF16, kind="Internal")
        ccd["v1i"] = nc.dram_tensor("v1i", [D, T], BF16, kind="Internal")
        ccd["v1o"] = nc.dram_tensor("v1o", [2 * D, T], BF16, kind="Internal")

    BASE = 17 * KB
    A0, A1, A2, ST = BASE, BASE + 64 * KB, BASE + 128 * KB, BASE + 192 * KB

    def al(name, shape, dt, off):
        return nc.alloc_sbuf_tensor_at(name, shape, dt, offset=off).ap(), off

    off = ST
    ropec = ropes = rt = None
    if do_l0:
        ropec, _ = al("ropec", [128, T], F32, off); off += 4 * KB
        ropes, _ = al("ropes", [128, T], F32, off); off += 4 * KB
        rt, _ = al("rt", [128, 128], BF16, off); off += 256
    maskkm, _ = al("maskkm", [128, 256], F32, off); off += KB
    maskqm, _ = al("maskqm", [128, 256], F32, off); off += KB
    alibic = None
    if do_l1:
        alibic, _ = al("alibic", [128, 256], F32, off); off += KB
    onescb, _ = al("onescb", [128, 1], BF16, off); off += 32
    onesrb, _ = al("onesrb", [1, 128], BF16, off); off += 256
    onesrf, _ = al("onesrf", [1, 128], F32, off); off += 512
    idb, _ = al("idb", [128, 128], BF16, off); off += 256
    b1r = b2r = b3r = gammar = bqkvr = bqs = boutr = None
    if do_l0:
        b1r, _ = al("b1r", [128, 32], F32, off); off += 128
        b2r, _ = al("b2r", [128, 32], F32, off); off += 128
        b3r, _ = al("b3r", [128, 16], F32, off); off += 64
        gammar, _ = al("gammar", [128, 16], F32, off); off += 64
    if do_l1:
        bqkvr, _ = al("bqkvr", [128, 48], F32, off); off += 192
        bqs, _ = al("bqs", [128, 16], F32, off); off += 64
        boutr, _ = al("boutr", [128, 16], F32, off); off += 64
    assert off <= 229376 - 1024, off

    xn, XN_O = al("xn", [128, DC * 1024], BF16, A2)
    q, Q_O = al("q", [128, H * 1024], BF16, A2 + 32 * KB)
    xn2 = h1 = h2 = inter = None
    XN2_O = H1_O = H2_O = INTER_O = 0
    if do_l0:
        xn2, XN2_O = al("xn2", [128, DC * 1024], BF16, A1)
        h1, H1_O = al("h1", [128, 32 * 1024], BF16, A0)
        h2, H2_O = al("h2", [128, 32 * 1024], BF16, A1)
    if do_l1:
        inter, INTER_O = al("inter", [128, NEXP * 4096], BF16, A1 + 32 * KB)
    attn_fm, AFM_O = al("attn_fm", [128, H * 1024], BF16, A1 + 32 * KB)
    g_sb, G_O = al("g", [128, DC * 1024], F32, A0)

    def albuf(pfx, n, shape, dt, off0, stride):
        r = []
        for i in range(n):
            a, o = al(f"{pfx}{i}", shape, dt, off0 + i * stride)
            r.append((a, o))
        return r

    wstA = albuf("wstA", 2, [128, 4096], BF16, A0, 8 * KB)
    wstB = albuf("wstB", 2, [128, 4096], BF16, A1, 8 * KB)
    wstC = albuf("wstC", 2, [128, 4096], BF16, A2, 8 * KB)

    xinA = albuf("xinA", 2, [128, 1024], F32, A0 + 16 * KB, 4 * KB)
    tfA = albuf("tfA", 2, [128, 512], F32, A0 + 24 * KB, 2 * KB)
    rsc = albuf("rsc", 2, [128, 512], F32, A0 + 28 * KB, 2 * KB)
    qtmp, QT_O = al("qtmp", [128, 512], BF16, A0 + 32 * KB)
    ktmp, KT_O = al("ktmp", [128, 1024], BF16, A0 + 33 * KB)
    kvt = albuf("kvt", 2, [128, 1024], BF16, A0 + 36 * KB, 2 * KB)

    xinB = albuf("xinB", 2, [128, 1024], F32, A1, 4 * KB)
    x1b = albuf("x1b", 2, [128, 1024], F32, A1 + 8 * KB, 4 * KB)
    tfB = albuf("tfB", 2, [128, 512], F32, A1 + 16 * KB, 2 * KB)

    kb_ = albuf("kbuf", 2, [128, 2048], BF16, A1, 4 * KB)
    vb_ = albuf("vbuf", 2, [128, 2048], BF16, A1 + 8 * KB, 4 * KB)
    psb = albuf("psb", 4, [128, 512], BF16, A1 + 16 * KB, KB)
    avbf = albuf("avbf", 2, [128, 512], BF16, A1 + 28 * KB, KB)
    gupst = albuf("gupst", 4, [128, 4096], BF16, A0, 8 * KB) if do_l1 else []

    pmax, PMAX_O = al("pmax", [128, 32], F32, A2 + 20 * KB)
    negbs, NEGBS_O = al("negbs", [128, 8], BF16, A2 + 20 * KB + 128)
    negrow, NEGROW_O = al("negrow", [1, 1024], BF16, A2 + 20 * KB + 256)
    invrow, INVROW_O = al("invrow", [1, 1024], F32, A2 + 20 * KB + 2304)
    arowb = albuf("arow", 2, [1, SEQ], BF16, A1 + 20 * KB, 4 * KB) if do_l1 else []
    xbf, XBF_O = al("xbf", [128, 1024], BF16, A2 + 16 * KB)
    sqbf, SQBF_O = al("sqbf", [128, 1024], BF16, A2 + 18 * KB)
    rows, ROWS_O = al("rows", [1, 4096], F32, A2 + 48 * KB)

    P = [nc.alloc_psum_tensor(f"P{i}", [128, 512], F32).ap() for i in range(8)]

    def PK(i):
        return [("PS", i)]

    xn3 = xn.rearrange("p (k t) -> p k t", k=DC)
    xn23 = xn2.rearrange("p (k t) -> p k t", k=DC) if do_l0 else None
    h13 = h1.rearrange("p (k t) -> p k t", k=32) if do_l0 else None
    h23 = h2.rearrange("p (k t) -> p k t", k=32) if do_l0 else None
    afm3 = attn_fm.rearrange("p (k t) -> p k t", k=H)
    g3 = g_sb.rearrange("p (k t) -> p k t", k=DC)
    int3 = inter.rearrange("p (e k t) -> p e k t", e=NEXP, k=IEXP // 128) if do_l1 else None
    q3 = q.rearrange("p (hh t) -> p hh t", hh=H)

    def dma(queue, out_ap, in_ap, rd, wr, tag=""):
        return sc.op(queue, lambda e, o=out_ap, i=in_ap: e.dma_start(out=o, in_=i),
                     rd=rd, wr=wr, kind="dma", tag=tag)

    def dma_t(out_ap, in_ap, rd, wr, tag=""):
        return sc.op("y", lambda e, o=out_ap, i=in_ap:
                     e.dma_start(out=o, in_=i, transpose=True),
                     rd=rd, wr=wr, kind="dma", tag=tag)

    def mm(out_ap, lhsT, rhs, rd, wr, start=True, stop=True, tag=""):
        return sc.op("t", lambda e, o=out_ap, l=lhsT, r=rhs, st=start, sp=stop:
                     e.matmul(o, l, r, start=st, stop=sp),
                     rd=rd, wr=wr, tag=tag)

    def act(out_ap, in_ap, func, rd, wr, bias=0.0, scale=1.0, accum=None, tag=""):
        return sc.op("s", lambda e, o=out_ap, i=in_ap, f=func, b=bias, sl=scale,
                     a=accum: e.activation(o, i, f, bias=b, scale=sl, accum_out=a),
                     rd=rd, wr=wr, tag=tag, hazard=True)

    def _hz(out_ap):
        return out_ap.free_size() <= 128

    def tt(out_ap, a_ap, b_ap, op, rd, wr, tag=""):
        return sc.op("v", lambda e, o=out_ap, a=a_ap, b=b_ap, p=op:
                     e.tensor_tensor(o, a, b, p), rd=rd, wr=wr, tag=tag,
                     hazard=_hz(out_ap))

    def ts(out_ap, in_ap, scalar, op, rd, wr, tag=""):
        return sc.op("v", lambda e, o=out_ap, i=in_ap, s_=scalar, p=op:
                     e.tensor_scalar(o, i, s_, None, p), rd=rd, wr=wr, tag=tag,
                     hazard=_hz(out_ap))

    def stt(out_ap, in0, scalar, in1, op0, op1, rd, wr, tag=""):
        return sc.op("v", lambda e, o=out_ap, i0=in0, s_=scalar, i1=in1, p0=op0,
                     p1=op1: e.scalar_tensor_tensor(o, i0, s_, i1, p0, p1),
                     rd=rd, wr=wr, tag=tag, hazard=_hz(out_ap))

    def vcopy(out_ap, in_ap, rd, wr, tag=""):
        return sc.op("v", lambda e, o=out_ap, i=in_ap: e.tensor_copy(o, i),
                     rd=rd, wr=wr, tag=tag, hazard=_hz(out_ap))

    def vredmax(out_ap, in_ap, rd, wr, tag="", negate=False):
        return sc.op("v", lambda e, o=out_ap, i=in_ap, n=negate:
                     e.reduce_max(o, i, axis=AX.X, negate=n), rd=rd, wr=wr,
                     tag=tag, hazard=True)

    def vrecip(out_ap, in_ap, rd, wr, tag=""):
        return sc.op("v", lambda e, o=out_ap, i=in_ap: e.reciprocal(o, i),
                     rd=rd, wr=wr, tag=tag, hazard=True)

    CONSTK = [("CONST", 0)]
    consts = [(maskkm, "maskkmd"), (maskqm, "maskqmd"), (onescb, "onescbd"),
              (onesrb, "onesrbd"), (onesrf, "onesrfd"), (idb, "idbd")]
    if do_l0:
        consts += [(ropec, "ropecd"), (ropes, "ropesd"), (rt, "rtd"),
                   (b1r, "b1r"), (b2r, "b2r"), (b3r, "b3r"), (gammar, "gammar")]
    if do_l1:
        consts += [(alibic, "alibicd"), (bqkvr, "bqkvr"), (boutr, "boutr")]
    for ap_, nm in consts:
        dma("g", ap_, w[nm], drk(nm), CONSTK, tag="const")
    if do_l1:
        ts(bqs, bqkvr[:, 0:16], QSCALE, ALU.mult, CONSTK, CONSTK, tag="bqs")

    def emit_stats_chunk(x_ap, x_keys, mc):
        act(xbf, x_ap, AF.Identity, x_keys, sbk(XBF_O, 2048), tag="xbf")
        tt(sqbf, xbf, xbf, ALU.mult, sbk(XBF_O, 2048), sbk(SQBF_O, 2048), tag="sq")
        for th in range(2):
            mm(P[4 + th][0:1, :], onescb, xbf[:, th * 512:(th + 1) * 512],
               sbk(XBF_O, 2048) + CONSTK, PK(4 + th),
               start=(mc == 0), stop=(mc == DC - 1), tag="s1")
            mm(P[6 + th][0:1, :], onescb, sqbf[:, th * 512:(th + 1) * 512],
               sbk(SQBF_O, 2048) + CONSTK, PK(6 + th),
               start=(mc == 0), stop=(mc == DC - 1), tag="s2")

    def emit_stats_from_dram(src, srcname):
        for mc in range(DC):
            xa, xo = xinA[mc % 2]
            dma("y", xa, src[mc * 128:(mc + 1) * 128, :],
                drk(srcname, mc * 128 * T * 4, 128 * T * 4), sbk(xo, 4096), tag="stx")
            emit_stats_chunk(xa, sbk(xo, 4096), mc)

    def emit_rowmath(l0):
        R = [rows[0:1, i * 1024:(i + 1) * 1024] for i in range(4)]
        RK = [sbk(ROWS_O + i * 4096, 4096) for i in range(4)]
        for th in range(2):
            vcopy(R[0][0:1, th * 512:(th + 1) * 512], P[4 + th][0:1, :],
                  PK(4 + th), RK[0], tag="s1cp")
            vcopy(R[1][0:1, th * 512:(th + 1) * 512], P[6 + th][0:1, :],
                  PK(6 + th), RK[1], tag="s2cp")
        ts(R[2], R[0], 1.0 / D, ALU.mult, RK[0], RK[2], tag="mu")
        ts(R[1], R[1], 1.0 / D, ALU.mult, RK[1], RK[1], tag="ms")
        tt(R[0], R[2], R[2], ALU.mult, RK[2], RK[0], tag="musq")
        tt(R[1], R[1], R[0], ALU.subtract, RK[0] + RK[1], RK[1], tag="var")
        ts(R[0], R[1], EPS, ALU.add, RK[1], RK[0], tag="vpe")
        act(R[0], R[0], AF.Ln, RK[0], RK[0], tag="ln")
        act(R[3], R[0], AF.Exp, RK[0], RK[3], scale=-0.5, tag="rstd")
        for th in range(2):
            mm(P[0 + th][:, :], onesrf, R[2][0:1, th * 512:(th + 1) * 512],
               RK[2] + CONSTK, PK(0 + th), tag="bmu")
            mm(P[2 + th][:, :], onesrf, R[3][0:1, th * 512:(th + 1) * 512],
               RK[3] + CONSTK, PK(2 + th), tag="brstd")
        if l0:
            tt(R[0], R[1], R[3], ALU.mult, RK[1] + RK[3], RK[0], tag="v2a")
            tt(R[0], R[0], R[3], ALU.mult, RK[0] + RK[3], RK[0], tag="v2b")
            ts(R[0], R[0], EPS, ALU.add, RK[0], RK[0], tag="v2c")
            act(R[0], R[0], AF.Ln, RK[0], RK[0], tag="ln2")
            act(R[0], R[0], AF.Exp, RK[0], RK[0], scale=-0.5, tag="rstd2")
            for th in range(2):
                mm(P[4 + th][:, :], onesrf, R[0][0:1, th * 512:(th + 1) * 512],
                   RK[0] + CONSTK, PK(4 + th), tag="brstd2")

    def emit_apply(src, srcname, l0):
        for mc in range(DC):
            xa, xo = xinA[mc % 2]
            dma("y", xa, src[mc * 128:(mc + 1) * 128, :],
                drk(srcname, mc * 128 * T * 4, 128 * T * 4), sbk(xo, 4096), tag="apx")
            for th in range(2):
                ta, to = tfA[th]
                tt(ta, xa[:, th * 512:(th + 1) * 512], P[0 + th][:, :], ALU.subtract,
                   sbk(xo, 4096) + PK(0 + th), sbk(to, 2048), tag="sub")
                tt(xn3[:, mc, th * 512:(th + 1) * 512], ta, P[2 + th][:, :], ALU.mult,
                   sbk(to, 2048) + PK(2 + th),
                   sbk(XN_O + mc * 2048 + th * 1024, 1024), tag="xn")
                if l0:
                    stt(xn23[:, mc, th * 512:(th + 1) * 512],
                        xn3[:, mc, th * 512:(th + 1) * 512],
                        gammar[:, mc:mc + 1], P[4 + th][:, :], ALU.mult, ALU.mult,
                        sbk(XN_O + mc * 2048 + th * 1024, 1024) + CONSTK + PK(4 + th),
                        sbk(XN2_O + mc * 2048 + th * 1024, 1024), tag="xn2")

    def xn_keys(k, t0, tn):
        return sbk(XN_O + k * 2048 + t0 * 2, tn * 2)

    def xn2_keys(k, t0, tn):
        return sbk(XN2_O + k * 2048 + t0 * 2, tn * 2)

    def afm_keys(k, t0, tn):
        return sbk(AFM_O + k * 2048 + t0 * 2, tn * 2)

    def wstream(wname, K, m0, m1, stripes, rhs3, rhs_keyfn, evac, mgroup=256,
                pbanks=(0, 1, 2, 3), row0=0):
        KC = K // 128
        rot = [0]
        wr3 = w[wname].rearrange("(k p) m -> p k m", p=128)
        kbase = row0 // 128
        for gi, g0 in enumerate(range(m0, m1, mgroup)):
            sa, so = stripes[gi % 2]
            s3 = sa.rearrange("p (k m) -> p k m", k=KC, m=mgroup)
            dma("g", s3, wr3[:, kbase:kbase + KC, g0:g0 + mgroup],
                drk(wname), sbk(so, 8192), tag="w:" + wname)
            for ml in range(mgroup // 128):
                mc_abs = (g0 + ml * 128) // 128
                for th in range(2):
                    t0, tn = th * 512, 512
                    pi = pbanks[rot[0] % len(pbanks)]
                    rot[0] += 1
                    for k in range(KC):
                        mm(P[pi][:, 0:tn], s3[:, k, ml * 128:(ml + 1) * 128],
                           rhs3[:, k, t0:t0 + tn],
                           sbk(so, 8192) + rhs_keyfn(k, t0, tn), PK(pi),
                           start=(k == 0), stop=(k == KC - 1), tag="mm:" + wname)
                    evac(mc_abs, th, t0, tn, pi)

    def emit_rope(src_ap, src_keys, dst_ap, dst_keys, pb):
        for th in range(2):
            pi = pb[th % len(pb)]
            mm(P[pi][:, :], rt, src_ap[:, th * 512:(th + 1) * 512],
               src_keys + CONSTK, PK(pi), tag="rot")
            ra, ro = rsc[0]
            tt(ra, src_ap[:, th * 512:(th + 1) * 512],
               ropec[:, th * 512:(th + 1) * 512], ALU.mult,
               src_keys + CONSTK, sbk(ro, 2048), tag="rc")
            ra2, ro2 = rsc[1]
            tt(ra2, P[pi][:, :], ropes[:, th * 512:(th + 1) * 512], ALU.mult,
               PK(pi) + CONSTK, sbk(ro2, 2048), tag="rs")
            tt(dst_ap[:, th * 512:(th + 1) * 512], ra, ra2, ALU.add,
               sbk(ro, 2048) + sbk(ro2, 2048), dst_keys, tag="radd")

    def emit_kvq_l0():
        sa, so = wstA[0]
        s3 = sa.rearrange("p (k m) -> p k m", k=DC, m=256)
        dma("g", s3, w["wkv"].rearrange("(k p) m -> p k m", p=128),
            drk("wkv"), sbk(so, 8192), tag="w:wkv")
        for th in range(2):
            pi = th
            for k in range(DC):
                mm(P[pi][:, :], s3[:, k, 0:128], xn23[:, k, th * 512:(th + 1) * 512],
                   sbk(so, 8192) + xn2_keys(k, th * 512, 512), PK(pi),
                   start=(k == 0), stop=(k == DC - 1), tag="mmk0")
            act(ktmp[:, th * 512:(th + 1) * 512], P[pi][:, :], AF.Identity,
                PK(pi), sbk(KT_O + th * 1024, 1024), tag="evk0")
        emit_rope(ktmp, sbk(KT_O, 2048), kvt[0][0], sbk(kvt[0][1], 2048), (2, 3))
        dma("y", ccd["k0i"].ap(), kvt[0][0], sbk(kvt[0][1], 2048),
            drk("k0i"), tag="k0w")
        for th in range(2):
            pi = 2 + th
            for k in range(DC):
                mm(P[pi][:, :], s3[:, k, 128:256], xn23[:, k, th * 512:(th + 1) * 512],
                   sbk(so, 8192) + xn2_keys(k, th * 512, 512), PK(pi),
                   start=(k == 0), stop=(k == DC - 1), tag="mmv0")
            act(kvt[1][0][:, th * 512:(th + 1) * 512], P[pi][:, :], AF.Identity,
                PK(pi), sbk(kvt[1][1] + th * 1024, 1024), tag="evv0")
        dma("y", ccd["v0i"].ap(), kvt[1][0], sbk(kvt[1][1], 2048),
            drk("v0i"), tag="v0w")

    def emit_q_l0():
        KC = DC
        wr3 = w["wq"].rearrange("(k p) m -> p k m", p=128)
        for gi, g0 in enumerate(range(0, D, 256)):
            sa, so = wstA[gi % 2]
            s3 = sa.rearrange("p (k m) -> p k m", k=KC, m=256)
            dma("g", s3, wr3[:, :, g0:g0 + 256], drk("wq"), sbk(so, 8192), tag="w:wq")
            for ml in range(2):
                h = (g0 + ml * 128) // 128
                for th in range(2):
                    pi = th
                    for k in range(KC):
                        mm(P[pi][:, :], s3[:, k, ml * 128:(ml + 1) * 128],
                           xn23[:, k, th * 512:(th + 1) * 512],
                           sbk(so, 8192) + xn2_keys(k, th * 512, 512), PK(pi),
                           start=(k == 0), stop=(k == KC - 1), tag="mmq0")
                    act(qtmp, P[pi][:, :], AF.Identity, PK(pi),
                        sbk(QT_O, 1024), scale=QSCALE, tag="evq0")
                    pi2 = 2 + th
                    mm(P[pi2][:, :], rt, qtmp, sbk(QT_O, 1024) + CONSTK, PK(pi2),
                       tag="rotq")
                    ra, ro = rsc[0]
                    tt(ra, qtmp, ropec[:, th * 512:(th + 1) * 512], ALU.mult,
                       sbk(QT_O, 1024) + CONSTK, sbk(ro, 2048), tag="rcq")
                    ra2, ro2 = rsc[1]
                    tt(ra2, P[pi2][:, :], ropes[:, th * 512:(th + 1) * 512], ALU.mult,
                       PK(pi2) + CONSTK, sbk(ro2, 2048), tag="rsq")
                    tt(q3[:, h, th * 512:(th + 1) * 512], ra, ra2, ALU.add,
                       sbk(ro, 2048) + sbk(ro2, 2048),
                       sbk(Q_O + h * 2048 + th * 1024, 1024), tag="raddq")

    def emit_kvq_l1():
        KC = DC
        wr3 = w["wqkvt"].rearrange("(k p) m -> p k m", p=128)
        for gi, g0 in enumerate(range(0, 3 * D, 256)):
            sa, so = wstA[gi % 2]
            s3 = sa.rearrange("p (k m) -> p k m", k=KC, m=256)
            dma("g", s3, wr3[:, :, g0:g0 + 256], drk("wqkvt"), sbk(so, 8192),
                tag="w:wqkvt")
            for ml in range(2):
                mc = (g0 + ml * 128) // 128
                kind = mc // 16
                mloc = mc % 16
                kv, kvo = kvt[mc % 2]
                for th in range(2):
                    pi = (mc * 2 + th) % 4
                    for k in range(KC):
                        mm(P[pi][:, :], s3[:, k, ml * 128:(ml + 1) * 128],
                           xn3[:, k, th * 512:(th + 1) * 512],
                           sbk(so, 8192) + xn_keys(k, th * 512, 512), PK(pi),
                           start=(k == 0), stop=(k == KC - 1), tag="mmqkv")
                    if kind == 0:
                        act(q3[:, mloc, th * 512:(th + 1) * 512], P[pi][:, :],
                            AF.Identity, PK(pi) + CONSTK,
                            sbk(Q_O + mloc * 2048 + th * 1024, 1024),
                            bias=bqs[:, mloc:mloc + 1], scale=QSCALE, tag="evq1")
                    else:
                        act(kv[:, th * 512:(th + 1) * 512], P[pi][:, :], AF.Identity,
                            PK(pi) + CONSTK, sbk(kvo + th * 1024, 1024),
                            bias=bqkvr[:, 16 * kind + mloc:16 * kind + mloc + 1],
                            tag="evkv1")
                if kind == 1:
                    dma("y", ccd["k1i"].ap()[mloc * 128:(mloc + 1) * 128, :], kv,
                        sbk(kvo, 2048), drk("k1i", mloc * 128 * T * 2, 128 * T * 2),
                        tag="k1w")
                elif kind == 2:
                    dma("y", ccd["v1i"].ap()[mloc * 128:(mloc + 1) * 128, :], kv,
                        sbk(kvo, 2048), drk("v1i", mloc * 128 * T * 2, 128 * T * 2),
                        tag="v1w")

    def emit_collective(ki, ko, vi, vo):
        for inm, onm in ((ki, ko), (vi, vo)):
            sc.op("g", lambda e, a=ccd[inm], b=ccd[onm]: e.collective_compute(
                "AllGather", ALU.bypass,
                replica_groups=[[0, 4], [1, 5], [2, 6], [3, 7]],
                ins=[a.ap().opt()], outs=[b.ap().opt()]),
                rd=drk(inm), wr=drk(onm), kind="cc", tag="cc:" + inm)

    def emit_ffn_l0():
        def ev_h1(mc, th, t0, tn, pi):
            act(h13[:, mc, t0:t0 + tn], P[pi][:, 0:tn], AF.Relu,
                PK(pi) + CONSTK, sbk(H1_O + mc * 2048 + t0 * 2, tn * 2),
                bias=b1r[:, mc:mc + 1], tag="evh1")
        wstream("w1t", D, 0, 4096, wstB, xn3, xn_keys, ev_h1,
                pbanks=(0, 1, 2, 3, 4, 5))

        def h1k(k, t0, tn):
            return sbk(H1_O + k * 2048 + t0 * 2, tn * 2)

        def ev_h2(mc, th, t0, tn, pi):
            act(h23[:, mc, t0:t0 + tn], P[pi][:, 0:tn], AF.Relu,
                PK(pi) + CONSTK, sbk(H2_O + mc * 2048 + t0 * 2, tn * 2),
                bias=b2r[:, mc:mc + 1], tag="evh2")
        wstream("w2t", 4096, 0, 4096, wstC, h13, h1k, ev_h2, mgroup=128,
                pbanks=(0, 1, 2, 3, 4, 5))

        def h2k(k, t0, tn):
            return sbk(H2_O + k * 2048 + t0 * 2, tn * 2)

        def ev_g(mc, th, t0, tn, pi):
            act(g3[:, mc, t0:t0 + tn], P[pi][:, 0:tn], AF.Silu,
                PK(pi) + CONSTK, sbk(G_O + mc * 4096 + t0 * 4, tn * 4),
                bias=b3r[:, mc:mc + 1], tag="evg")
        wstream("w3t", 4096, 0, D, wstC, h23, h2k, ev_g, mgroup=128,
                pbanks=(0, 1, 2, 3, 4, 5))

    def emit_ffn_l1():
        IC = IEXP // 128
        wg3 = w["gup"].rearrange("(k p) m -> p k m", p=128)
        for e_ in range(NEXP):
            for gi, j0 in enumerate(range(0, IEXP, 256)):
                sa, so = gupst[2 * (gi % 2)] if FFN_NEWBUF else wstB[0]
                sau, sou = gupst[2 * (gi % 2) + 1] if FFN_NEWBUF else wstB[1]
                sg3 = sa.rearrange("p (k m) -> p k m", k=DC, m=256)
                su3 = sau.rearrange("p (k m) -> p k m", k=DC, m=256)
                dma("g", sg3, wg3[:, e_ * DC:(e_ + 1) * DC, j0:j0 + 256],
                    drk("gup"), sbk(so, 8192), tag="w:gupg")
                dma("g", su3, wg3[:, e_ * DC:(e_ + 1) * DC,
                               IEXP + j0:IEXP + j0 + 256],
                    drk("gup"), sbk(sou, 8192), tag="w:gupu")
                for jl in range(2):
                    jc = (j0 + jl * 128) // 128
                    pg, pu = (((0, 1), (2, 3), (4, 5), (6, 7))[jc % 4]
                              if FFN_NEW else ((0, 1), (2, 3))[jc % 2])
                    for k in range(DC):
                        mm(P[pg][:, 0:256], sg3[:, k, jl * 128:(jl + 1) * 128],
                           xn3[:, k, e_::4],
                           sbk(so, 8192) + xn_keys(k, 0, 1024), PK(pg),
                           start=(k == 0), stop=(k == DC - 1), tag="mmgate")
                    for k in range(DC):
                        mm(P[pu][:, 0:256], su3[:, k, jl * 128:(jl + 1) * 128],
                           xn3[:, k, e_::4],
                           sbk(sou, 8192) + xn_keys(k, 0, 1024), PK(pu),
                           start=(k == 0), stop=(k == DC - 1), tag="mmup")
                    pb, pbo = psb[0]
                    act(pb[:, 0:256], P[pg][:, 0:256], AF.Silu, PK(pg),
                        sbk(pbo, 512), tag="evgate")
                    tt(int3[:, e_, jc, :], pb[:, 0:256], P[pu][:, 0:256], ALU.mult,
                       sbk(pbo, 512) + PK(pu),
                       sbk(INTER_O + e_ * 8192 + jc * 512, 512), tag="inter")
        wd3 = w["dwn"].rearrange("(k p) m -> p k m", p=128)
        for e_ in range(NEXP):
            for gi, m0 in enumerate(range(0, D, 256)):
                sa, so = wstB[gi % 2]
                s3 = sa.rearrange("p (k m) -> p k m", k=IC, m=256)
                dma("g", s3, wd3[:, e_ * IC:(e_ + 1) * IC, m0:m0 + 256],
                    drk("dwn"), sbk(so, 8192), tag="w:dwn")
                for ml in range(2):
                    mc = (m0 + ml * 128) // 128
                    pi = mc % 4
                    for k in range(IC):
                        mm(P[pi][:, 0:256], s3[:, k, ml * 128:(ml + 1) * 128],
                           int3[:, e_, k, :],
                           sbk(so, 8192) + sbk(INTER_O + e_ * 8192 + k * 512, 512),
                           PK(pi), start=(k == 0), stop=(k == IC - 1), tag="mmdwn")
                    act(g3[:, mc, e_::4], P[pi][:, 0:256], AF.Silu,
                        PK(pi), sbk(G_O + mc * 4096, 4096), tag="evg1")

    def emit_attention(layer):
        l1 = layer == 1
        kon = "k1o" if l1 else "k0o"
        von = "v1o" if l1 else "v0o"
        ko = ccd[kon].ap()
        vo = ccd[von].ap()
        NEGROWK = sbk(NEGROW_O, 2048)
        INVK = sbk(INVROW_O, 4096)

        def load_kv(h, kbuf, kko, vbuf, vko):
            for r in range(2):
                src = (ko[r * D + h * 128:r * D + h * 128 + 128, :] if l1
                       else ko[r * 128:(r + 1) * 128, :])
                dma("g", kbuf[:, r * 1024:(r + 1) * 1024], src,
                    drk(kon), sbk(kko + r * 2048, 2048), tag="kld")
            for c in range(16):
                s_k, r = c // 2, c % 2
                src = (vo[r * D + h * 128:r * D + h * 128 + 128,
                          s_k * 128:(s_k + 1) * 128] if l1
                       else vo[r * 128:(r + 1) * 128, s_k * 128:(s_k + 1) * 128])
                sc.op("y" if c % 2 == 0 else "s",
                      lambda e, o=vbuf[:, c * 128:(c + 1) * 128], i=src:
                      e.dma_start(out=o, in_=i, transpose=True),
                      rd=drk(von), wr=sbk(vko + c * 256, 256), kind="dma",
                      tag="vld")

        if not l1:
            load_kv(0, kb_[0][0], kb_[0][1], vb_[0][0], vb_[0][1])

        for h in range(H):
            if l1:
                kbuf, kko = kb_[h % 2]
                vbuf, vko = vb_[h % 2]
                load_kv(h, kbuf, kko, vbuf, vko)
                ab, abo = arowb[h % 2]
                dma("g", ab, w["arowd"][h:h + 1, :], drk("arowd"),
                    sbk(abo, 4096), tag="arow")
            else:
                kbuf, kko = kb_[0]
                vbuf, vko = vb_[0]
            KBK = sbk(kko, 4096)
            VBK = sbk(vko, 4096)

            qrot = [0]
            for s in range(8):
                qsl = q3[:, h, s * 128:(s + 1) * 128]
                QSK = sbk(Q_O + h * 2048 + s * 256, 256)
                np_ = 0
                for r in range(2):
                    cap = (s + 1) * 128
                    for k0 in range(0, cap, 512):
                        kn = min(512, cap - k0)
                        pi = (0, 1, 2, 3)[qrot[0] % 4]
                        qrot[0] += 1
                        mm(P[pi][:, 0:kn], qsl,
                           kbuf[:, r * 1024 + k0:r * 1024 + k0 + kn],
                           QSK + KBK, PK(pi), start=True, stop=not l1, tag="qmj")
                        if l1:
                            mm(P[pi][:, 0:kn], onesrb,
                               ab[0:1, r * 1024 + k0:r * 1024 + k0 + kn],
                               CONSTK + sbk(abo, 4096), PK(pi),
                               start=False, stop=True, tag="qmja")
                        if k0 + kn == cap:
                            tt(P[pi][:, kn - 128:kn], P[pi][:, kn - 128:kn],
                               maskqm[:, r * 128:(r + 1) * 128], ALU.add,
                               PK(pi) + CONSTK, PK(pi), tag="qmmask")
                        col = s * 4 + np_
                        np_ += 1
                        vredmax(pmax[:, col:col + 1], P[pi][:, 0:kn], PK(pi),
                                sbk(PMAX_O, 128), tag="qmred")
                vredmax(negbs[:, s:s + 1], pmax[:, s * 4:s * 4 + np_],
                        sbk(PMAX_O, 128), sbk(NEGBS_O, 16), tag="qmneg",
                        negate=True)
                if DEBUG_DUMP and l1 and h == 0 and s == 0:
                    for rr_ in range(2):
                        dq = nc.dram_tensor(f"dbg_qmj{rr_}", [128, 128], F32,
                                            kind="ExternalOutput").ap()
                        scr = nc.alloc_sbuf_tensor_at(
                            f"dbg_scr{rr_}", [128, 128], F32,
                            offset=A1 + 24 * KB + rr_ * 512).ap()
                        act(scr, P[rr_][:, 0:128], AF.Identity, PK(rr_),
                            sbk(A1 + 24 * KB + rr_ * 512, 512), tag="dbgev")
                        dma("y", dq, scr, sbk(A1 + 24 * KB + rr_ * 512, 512),
                            drk(f"dbg_qmj{rr_}"), tag="dbgq")

            for s in range(8):
                mm(P[6 + s // 4][0:1, (s % 4) * 128:(s % 4) * 128 + 128],
                   negbs[:, s:s + 1], idb, sbk(NEGBS_O, 16) + CONSTK,
                   PK(6 + s // 4), start=True, stop=True, tag="negt")
            for hf in range(2):
                act(negrow[0:1, hf * 512:(hf + 1) * 512], P[6 + hf][0:1, 0:512],
                    AF.Identity, PK(6 + hf), NEGROWK, tag="negev")

            if DEBUG_DUMP and l1 and h == 0:
                dnr = nc.dram_tensor("dbg_negrow", [1, 1024], BF16,
                                     kind="ExternalOutput").ap()
                dma("y", dnr, negrow, NEGROWK, drk("dbg_negrow"), tag="dbgnr")

            krot = [0]
            for c in range(16):
                s_k, r = c // 2, c % 2
                base = r * 1024 + s_k * 128
                q0full = s_k * 128
                pieces = []
                p0 = q0full
                while p0 < 1024:
                    pn = min(512 - p0 % 512, 1024 - p0)
                    pieces.append((p0, pn))
                    p0 += pn
                for (q0, pn) in pieces:
                    pi = (0, 1, 6, 7)[krot[0] % KM_NBANK]
                    krot[0] += 1
                    mm(P[pi][:, 0:pn], kbuf[:, base:base + 128],
                       q3[:, h, q0:q0 + pn],
                       KBK + sbk(Q_O + h * 2048 + q0 * 2, pn * 2), PK(pi),
                       start=True, stop=False, tag="kmj")
                    mm(P[pi][:, 0:pn], onesrb, negrow[0:1, q0:q0 + pn],
                       CONSTK + NEGROWK, PK(pi), start=False, stop=True, tag="kmjm")
                    if q0 == q0full:
                        tt(P[pi][:, 0:128], P[pi][:, 0:128],
                           maskkm[:, r * 128:(r + 1) * 128], ALU.add,
                           PK(pi) + CONSTK, PK(pi), tag="kmmask")
                    pb, pbo = psb[(krot[0] - 1) % KM_NBANK]
                    bias = alibic[:, h * 16 + c:h * 16 + c + 1] if l1 else 0.0
                    act(pb[:, 0:pn], P[pi][:, 0:pn], AF.Exp, PK(pi) + CONSTK,
                        sbk(pbo, 1024), bias=bias, tag="exp")
                    half = q0 // 512
                    qq = q0 % 512
                    mm(P[4 + half][0:1, qq:qq + pn], onescb, pb[:, 0:pn],
                       CONSTK + sbk(pbo, 1024), PK(4 + half),
                       start=(c == 0), stop=(c == 15), tag="rsum")
                    mm(P[2 + half][:, qq:qq + pn], vbuf[:, c * 128:(c + 1) * 128],
                       pb[:, 0:pn], VBK + sbk(pbo, 1024), PK(2 + half),
                       start=(c == 0), stop=(c == 15), tag="av")
            for half in range(2):
                act(invrow[0:1, half * 512:(half + 1) * 512], P[4 + half][0:1, :],
                    AF.Ln, PK(4 + half), INVK, tag="lnsum")
                act(invrow[0:1, half * 512:(half + 1) * 512],
                    invrow[0:1, half * 512:(half + 1) * 512],
                    AF.Exp, INVK, INVK, scale=-1.0, tag="invexp")
                ab_, abo_ = avbf[half]
                act(ab_, P[2 + half][:, :], AF.Identity, PK(2 + half),
                    sbk(abo_, 1024), tag="avev")
                mm(P[2 + half][:, :], onesrf, invrow[0:1, half * 512:(half + 1) * 512],
                   CONSTK + INVK, PK(2 + half), tag="binv")
                tt(afm3[:, h, half * 512:(half + 1) * 512], ab_, P[2 + half][:, :],
                   ALU.mult, sbk(abo_, 1024) + PK(2 + half),
                   sbk(AFM_O + h * 2048 + half * 1024, 1024), tag="aout")
            if DEBUG_DUMP and l1 and h == 0:
                dir_ = nc.dram_tensor("dbg_invrow", [1, 1024], F32,
                                      kind="ExternalOutput").ap()
                dma("y", dir_, invrow, INVK, drk("dbg_invrow"), tag="dbgir")

    def emit_combine(layer, x_src, srcname, x1_dst, dstname, fuse_stats):
        l1 = layer == 1
        wname = "woutt" if l1 else "wo"
        KC = DC
        wr3 = w[wname].rearrange("(k p) m -> p k m", p=128)
        for gi, g0 in enumerate(range(0, D, 256)):
            sa, so = wstC[gi % 2]
            s3 = sa.rearrange("p (k m) -> p k m", k=KC, m=256)
            dma("g", s3, wr3[:, :, g0:g0 + 256], drk(wname), sbk(so, 8192),
                tag="w:" + wname)
            for ml in range(2):
                mc = (g0 + ml * 128) // 128
                xa, xo = xinB[mc % 2]
                dma("y", xa, x_src[mc * 128:(mc + 1) * 128, :],
                    drk(srcname, mc * 128 * T * 4, 128 * T * 4), sbk(xo, 4096),
                    tag="cbx")
                xb, xbo = x1b[mc % 2]
                for th in range(2):
                    pi = (mc * 2 + th) % 4
                    for k in range(KC):
                        mm(P[pi][:, :], s3[:, k, ml * 128:(ml + 1) * 128],
                           afm3[:, k, th * 512:(th + 1) * 512],
                           sbk(so, 8192) + afm_keys(k, th * 512, 512), PK(pi),
                           start=(k == 0), stop=(k == KC - 1), tag="mmwo")
                    ta, to = tfB[th]
                    bias = boutr[:, mc:mc + 1] if l1 else 0.0
                    stt(ta, P[pi][:, :], bias, g3[:, mc, th * 512:(th + 1) * 512],
                        ALU.add, ALU.mult,
                        PK(pi) + CONSTK + sbk(G_O + mc * 4096 + th * 2048, 2048),
                        sbk(to, 2048), tag="sg")
                    tt(xb[:, th * 512:(th + 1) * 512], ta,
                       xa[:, th * 512:(th + 1) * 512], ALU.add,
                       sbk(to, 2048) + sbk(xo, 4096),
                       sbk(xbo + th * 2048, 2048), tag="x1")
                dma("y", x1_dst[mc * 128:(mc + 1) * 128, :], xb, sbk(xbo, 4096),
                    drk(dstname, mc * 128 * T * 4, 128 * T * 4), tag="x1w")
                if fuse_stats:
                    emit_stats_chunk(xb, sbk(xbo, 4096), mc)

    if do_l0:
        sc.phase("L0stats"); emit_stats_from_dram(x_in, "x")
        sc.phase("L0rows"); emit_rowmath(l0=True)
        sc.phase("L0apply"); emit_apply(x_in, "x", l0=True)
        sc.phase("L0kv"); emit_kvq_l0()
        sc.phase("L0cc"); emit_collective("k0i", "k0o", "v0i", "v0o")
        sc.phase("L0q"); emit_q_l0()
        sc.phase("L0ffn"); emit_ffn_l0()
        sc.phase("L0attn"); emit_attention(0)
        sc.phase("L0comb"); emit_combine(0, x_in, "x", xmid, "xmid", fuse_stats=(mode == "fused"))
    if do_l1:
        if mode == "l1":
            emit_stats_from_dram(xmid, "xmid")
        sc.phase("L1rows"); emit_rowmath(l0=False)
        sc.phase("L1apply"); emit_apply(xmid, "xmid", l0=False)
        sc.phase("L1kv"); emit_kvq_l1()
        sc.phase("L1cc"); emit_collective("k1i", "k1o", "v1i", "v1o")
        sc.phase("L1ffn"); emit_ffn_l1()
        sc.phase("L1attn"); emit_attention(1)
        sc.phase("L1comb"); emit_combine(1, xmid, "xmid", out, "out", fuse_stats=False)

    if DEBUG_DUMP:
        dumps = [("xn", xn, BF16, DC * 1024, XN_O),
                 ("q", q, BF16, H * 1024, Q_O),
                 ("attn_fm", attn_fm, BF16, H * 1024, AFM_O),
                 ("g", g_sb, F32, DC * 1024, G_O)]
        for nm, ap_, dt, ncols, o_ in dumps:
            dd = nc.dram_tensor("dbg_" + nm, [128, ncols], dt,
                                kind="ExternalOutput").ap()
            dma("y", dd, ap_, sbk(o_, ncols * (2 if dt == BF16 else 4)),
                drk("dbg_" + nm), tag="dbg")

    from contextlib import ExitStack
    with ExitStack() as stack:
        sems = {}
        spaces = ["t", "v", "s", "g", "y", "cc"]
        for q in ("gq", "yq", "sq"):
            spaces += [f"{q}{i}" for i in range(NQSEM)]
        for sp in spaces:
            sems[sp] = stack.enter_context(nc.semaphore(sp))
        block = stack.enter_context(nc.Block())
        body = sc.make_bodies(sems)
        block.tensor(body("t", nc))
        block.vector(body("v", nc))
        block.scalar(body("s", nc))
        block.gpsimd(body("g", nc))
        block.sync(body("y", nc))
    nc._sched = sc
    return nc


def _prep_shared(inp):
    f = np.float32
    sh = {}
    sh["wq"] = np.ascontiguousarray(inp["L0_attn_Wq"], f)
    sh["wkv"] = np.ascontiguousarray(inp["L0_attn_Wkv"], f)
    sh["wo"] = np.ascontiguousarray(inp["L0_attn_Wo"], f)
    sh["w1t"] = np.ascontiguousarray(np.asarray(inp["L0_ffn_W1"], f).T)
    sh["w2t"] = np.ascontiguousarray(np.asarray(inp["L0_ffn_W2"], f).T)
    sh["w3t"] = np.ascontiguousarray(np.asarray(inp["L0_ffn_W3"], f).T)
    sh["b1r"] = np.ascontiguousarray(np.asarray(inp["L0_ffn_b1"], f).reshape(32, 128).T)
    sh["b2r"] = np.ascontiguousarray(np.asarray(inp["L0_ffn_b2"], f).reshape(32, 128).T)
    sh["b3r"] = np.ascontiguousarray(np.asarray(inp["L0_ffn_b3"], f).reshape(16, 128).T)
    sh["gammar"] = np.ascontiguousarray(np.asarray(inp["L0_attn_gamma"], f).reshape(16, 128).T)
    sh["wqkvt"] = np.ascontiguousarray(np.asarray(inp["L1_attn_Wqkv"], f).T)
    sh["woutt"] = np.ascontiguousarray(np.asarray(inp["L1_attn_Wout"], f).T)
    sh["gup"] = np.ascontiguousarray(np.asarray(inp["L1_ffn_gate_up_proj"], f)
                                     .reshape(NEXP * D, 2 * IEXP))
    sh["dwn"] = np.ascontiguousarray(np.asarray(inp["L1_ffn_down_proj"], f)
                                     .reshape(NEXP * IEXP, D))
    sh["bqkvr"] = np.ascontiguousarray(np.asarray(inp["L1_attn_b_qkv"], f).reshape(48, 128).T)
    sh["boutr"] = np.ascontiguousarray(np.asarray(inp["L1_attn_b_out"], f).reshape(16, 128).T)

    rtm = np.zeros((128, 128), f)
    for jj in range(64):
        rtm[64 + jj, jj] = -1.0
    for jj in range(64, 128):
        rtm[jj - 64, jj] = 1.0
    sh["rtd"] = rtm.astype(BF)

    sh["onescbd"] = np.ones((128, 1), BF)
    sh["onesrbd"] = np.ones((1, 128), BF)
    sh["onesrfd"] = np.ones((1, 128), f)
    sh["idbd"] = np.eye(128, dtype=BF)

    hh = np.arange(H)
    slopes = 2.0 ** (-(hh + 1) * 0.5)
    cpos = np.zeros((128, H * 16), f)
    for h in range(H):
        for c in range(16):
            s_k, r = c // 2, c % 2
            pos = (2 * s_k + r) * 128 + np.arange(128)
            cpos[:, h * 16 + c] = ((pos - (SEQ - 1)) * slopes[h]).astype(f)
    sh["alibicd"] = cpos
    ar = np.zeros((H, SEQ), np.float64)
    col = np.zeros(SEQ, np.int64)
    for r in range(2):
        for s in range(8):
            col[r * 1024 + s * 128:r * 1024 + (s + 1) * 128] = \
                (2 * s + r) * 128 + np.arange(128)
    for h in range(H):
        ar[h] = (col - (SEQ - 1)) * slopes[h]
    sh["arowd"] = ar.astype(BF)
    return sh


def _prep_core(inp, c):
    f = np.float32
    b, par = c % 4, c // 4
    tiles = [2 * s + par for s in range(8)]
    xs = np.asarray(inp["L0_attn_x"][b], f)
    xcore = np.ascontiguousarray(
        np.concatenate([xs[t * 128:(t + 1) * 128] for t in tiles], 0).T)
    d = {"x": xcore}
    pos = np.concatenate([np.arange(t * 128, (t + 1) * 128) for t in tiles])
    inv = 1.0 / (10000.0 ** (np.arange(0, DH, 2) / DH))
    fr = pos[:, None] * inv[None, :]
    fr = np.concatenate([fr, fr], 1)
    d["ropecd"] = np.ascontiguousarray(np.cos(fr).T.astype(f))
    d["ropesd"] = np.ascontiguousarray(np.sin(fr).T.astype(f))
    i_ = np.arange(128)[:, None]
    j_ = np.arange(128)[None, :]
    tri = np.where(i_ <= j_, 0.0, NEG).astype(f)
    zer = np.zeros((128, 128), f)
    full = np.full((128, 128), NEG, f)
    if par == 0:
        km = np.concatenate([tri, full], 1)
        qm = np.concatenate([np.ascontiguousarray(tri.T), full], 1)
    else:
        km = np.concatenate([zer, tri], 1)
        qm = np.concatenate([zer, np.ascontiguousarray(tri.T)], 1)
    d["maskkmd"] = np.ascontiguousarray(km)
    d["maskqmd"] = np.ascontiguousarray(qm)
    return d, b, tiles


def kernel(**inputs):
    shared = _prep_shared(inputs)
    per_core, metas = [], []
    for c in range(8):
        d, b, tiles = _prep_core(inputs, c)
        d.update(shared)
        per_core.append(d)
        metas.append((b, tiles))

    outfull = np.zeros((4, SEQ, D), np.float32)

    def scatter(res, key):
        for c in range(8):
            o = np.asarray(res.results[c][key]).T
            b, tiles = metas[c]
            for si, t in enumerate(tiles):
                outfull[b, t * 128:(t + 1) * 128, :] = o[si * 128:(si + 1) * 128]

    if MODE == "fused":
        nc = build_program("fused")
        res = run_bass_kernel_spmd(nc, per_core, core_ids=list(range(8)))
        scatter(res, "out")
    else:
        nc0 = build_program("l0")
        res0 = run_bass_kernel_spmd(nc0, per_core, core_ids=list(range(8)))
        nc1 = build_program("l1")
        maps1 = []
        for c in range(8):
            d = dict(per_core[c])
            d["xmid"] = np.ascontiguousarray(res0.results[c]["xmid"])
            maps1.append(d)
        res1 = run_bass_kernel_spmd(nc1, maps1, core_ids=list(range(8)))
        scatter(res1, "out")
    return outfull
